# version 10
# baseline (speedup 1.0000x reference)
import math
import numpy as np
import ml_dtypes
import concourse.bass as bass
import concourse.mybir as mybir
from concourse.alu_op_type import AluOpType
from contextlib import ExitStack

F32 = mybir.dt.float32
BF16 = mybir.dt.bfloat16
AF = mybir.ActivationFunctionType
AX = mybir.AxisListType
OP = AluOpType
BF16NP = ml_dtypes.bfloat16

NEG = -30000.0


class Cfg:
    def __init__(self, D=2048, TOKS=1024, DFF=8192, KVB=4, stages=("a", "f1", "b", "f2"),
                 n_cores=8):
        self.D = D
        self.H = D // 128
        self.DH = 128
        self.TOKS = TOKS
        self.SEQ = 2 * TOKS
        self.NT = TOKS // 128
        self.NCH = self.SEQ // 128
        self.NCHO = TOKS // 128
        self.DFF = DFF
        self.KVB = KVB
        self.DKV = KVB * 128
        self.KD = D // 128
        self.stages = stages
        self.n_cores = n_cores
        self.NLEV = 3
        self.EPS_SM = 1e-6
        self.LN_EPS = 1e-5
        self.SLAB = min(2048, DFF)
        self.NSLAB = DFF // self.SLAB
        self.OW = min(512, TOKS)
        self.NTC = TOKS // self.OW


class Sem:
    __slots__ = ("h", "n", "name")

    def __init__(self, h, name):
        self.h = h
        self.n = 0
        self.name = name

    def inc(self, k=1):
        self.n += k
        return self.n


class Buf:
    __slots__ = ("name", "writers", "readers")

    def __init__(self, name):
        self.name = name
        self.writers = []
        self.readers = []


def _merge(waits):
    d = {}
    for s, v in waits:
        if v > d.get(s, -1):
            d[s] = v
    return tuple(d.items())


class MK:
    ENGS = ("sync", "vector", "scalar", "tensor", "gpsimd")

    def __init__(self, cfg: Cfg):
        self.c = cfg
        self.nc = bass.Bass("TRN2", target_bir_lowering=False)
        self.ops = {e: [] for e in self.ENGS}
        self.ctx = ExitStack()
        self._sb = {}
        self._sems = {}
        self.es = {e: self.sem("es_" + e) for e in ("vector", "scalar", "tensor", "gpsimd")}
        self.bufs = {}
        self.wm = {e: {} for e in self.ENGS}

    def sem(self, name) -> Sem:
        if name not in self._sems:
            h = self.ctx.enter_context(self.nc.semaphore(name))
            self._sems[name] = Sem(h, name)
        return self._sems[name]

    def sb(self, name, shape, dtype):
        if name not in self._sb:
            self._sb[name] = self.ctx.enter_context(self.nc.sbuf_tensor(name, shape, dtype))
        return self._sb[name]

    def ps(self, name, shape, dtype=F32):
        if name not in self._sb:
            self._sb[name] = self.ctx.enter_context(self.nc.psum_tensor(name, shape, dtype))
        return self._sb[name]

    def buf(self, name) -> Buf:
        if name not in self.bufs:
            self.bufs[name] = Buf(name)
        return self.bufs[name]

    def _emit(self, eng, fn, sem, inc, waits, default_inc=False):
        waits = _merge(waits)
        wm = self.wm[eng]
        waits = tuple((s, v) for s, v in waits if v > wm.get(s, -1))
        for s, v in waits:
            wm[s] = v

        def run(e, fn=fn, waits=waits, sem=sem, inc=inc, default_inc=default_inc):
            for s, v in waits:
                e.wait_ge(s.h, v)
            inst = fn(e)
            if inst is not None and sem is not None:
                if default_inc:
                    inst.then_inc(sem.h)
                else:
                    inst.then_inc(sem.h, inc)
        self.ops[eng].append(run)

    def op(self, eng, fn, reads=(), writes=(), xw=(), sem=None, inc=1, default_inc=False,
           no_track_w=False):
        if sem is None:
            sem = self.es[eng]
        waits = list(xw)
        for b in reads:
            waits += b.writers
        for b in writes:
            waits += b.writers + b.readers
        n = sem.inc(inc)
        self._emit(eng, fn, sem, inc, waits, default_inc)
        for b in reads:
            b.readers.append((sem, n))
        if not no_track_w:
            for b in writes:
                b.writers = [(sem, n)]
                b.readers = []
        return sem, n

    def dma(self, q, out, in_, sem, reads=(), writes=(), xw=()):
        return self.op(q, lambda e, o=out, i=in_: e.dma_start(out=o, in_=i),
                       reads=reads, writes=writes, xw=xw, sem=sem, inc=16)

    def mm(self, out_ap, a_ap, b_ap, start, stop, rbufs, wbuf, xw=()):
        if start:
            waits = list(xw) + wbuf.writers + wbuf.readers
        else:
            waits = list(xw)
        for b in rbufs:
            waits += b.writers
        sem = self.es["tensor"]
        n = sem.inc(1)
        self._emit("tensor",
                   lambda e, o=out_ap, a=a_ap, bb=b_ap, st=start, sp=stop:
                   e.matmul(o, a, bb, start=st, stop=sp),
                   sem, 1, waits)
        for b in rbufs:
            b.readers.append((sem, n))
        if stop:
            wbuf.writers = [(sem, n)]
            wbuf.readers = []
        return sem, n

    def tp(self, out_ap, in_ap, ident_ap, rbufs, wbuf, xw=()):
        return self.op("tensor", lambda e, o=out_ap, i=in_ap, d=ident_ap: e.transpose(o, i, d),
                       reads=rbufs, writes=(wbuf,), xw=xw)

    def act(self, out_ap, in_ap, func, reads, writes, bias=None, scale=None, accum=None, xw=()):
        kw = {}
        if bias is not None:
            kw["bias"] = bias
        if scale is not None:
            kw["scale"] = scale
        if accum is not None:
            kw["accum_out"] = accum
        return self.op("scalar", lambda e, o=out_ap, i=in_ap, k=kw: e.activation(o, i, func, **k),
                       reads=reads, writes=writes, xw=xw)

    def vop(self, name, args, reads, writes, xw=(), kw=None):
        kw = kw or {}
        return self.op("vector", lambda e, a=args, k=kw: getattr(e, name)(*a, **k),
                       reads=reads, writes=writes, xw=xw)

    def collective(self, kind, aop, groups, in_ap, out_ap, rbuf, wbuf, name):
        s = self.sem("cc_" + name)
        return self.op("gpsimd",
                       lambda e, k=kind, o=aop, g=groups, i=in_ap, u=out_ap:
                       e.collective_compute(k, o, replica_groups=g, ins=[i], outs=[u]),
                       reads=(rbuf,), writes=(wbuf,), sem=s, inc=1, default_inc=True)

    def barrier_wait(self, eng, pairs):
        self._emit(eng, lambda e: None, None, 0, _merge(pairs))

    def build(self):
        nc = self.nc
        block = self.ctx.enter_context(nc.Block())
        for eng in self.ENGS:
            lst = self.ops[eng]

            def make(lst):
                def body(e):
                    for f in lst:
                        f(e)
                return body
            getattr(block, eng)(make(lst))
        self.ctx.close()
        return nc



def emit_ln_xnT(k: MK, co, x_dram_ap, xd_buf):
    c = k.c
    xnT = k.sb("A_xnT", [128, max(c.KD * c.TOKS, 8 * c.D)], BF16)
    bxnT = k.buf("A_xnT")
    ident = k.sb("ident_bf", [128, 128], BF16)
    bident = k.buf("ident_bf")
    xt = [k.sb("ln_xt0", [128, max(2048, c.D)], F32),
          k.sb("ln_xt1", [128, max(2048, c.D)], F32)]
    bxt = [k.buf("ln_xt0"), k.buf("ln_xt1")]
    sxt = [k.sem("s_lnx0"), k.sem("s_lnx1")]
    xnst = [k.sb("ln_st0", [128, c.D], BF16)] * 2
    bxnst = [k.buf("ln_st0")] * 2
    nseg = (c.D + 511) // 512
    stats = k.sb("ln_stats", [128, 6 * nseg], F32)
    bstats = k.buf("ln_stats")
    mv = k.sb("ln_mv", [128, 2], F32)
    bmv = k.buf("ln_mv")
    srt = k.sb("ln_srt", [128, 1], F32)
    bsrt = k.buf("ln_srt")
    rstd = k.sb("ln_rstd", [128, 1], F32)
    brstd = k.buf("ln_rstd")
    nmr = k.sb("ln_nmr", [128, 1], F32)
    bnmr = k.buf("ln_nmr")
    pb7 = k.ps("pb7", [128, 512], F32)
    pbt = pb7[:].bitcast(BF16)
    bpb7 = k.buf("pb7")

    vsem, tsem = k.es["vector"], k.es["tensor"]
    for t in range(c.NT):
        s = t % 2
        k.dma("sync", xt[s][:, 0:c.D], x_dram_ap[t * 128:(t + 1) * 128, :], sxt[s],
              reads=(xd_buf,), writes=(bxt[s],))
        last = None
        for g in range(nseg):
            g1 = min(512 * (g + 1), c.D)
            last = k.vop("bn_stats", (stats[:, 6 * g:6 * (g + 1)],
                                      xt[s][:, 512 * g:g1]),
                         reads=(bxt[s],), writes=(), xw=(bstats.writers + bstats.readers) if g == 0 else ())
        bstats.writers = [last]
        bstats.readers = []
        k.vop("bn_aggr", (mv[:], stats[:]), reads=(bstats,), writes=(bmv,))
        k.act(srt[:], mv[:, 1:2], AF.Sqrt, reads=(bmv, co["bcf"]), writes=(bsrt,),
              bias=co["epscol"])
        k.vop("reciprocal", (rstd[:], srt[:]), reads=(bsrt,), writes=(brstd,))
        k.vop("scalar_tensor_tensor", (nmr[:], mv[:, 0:1], -1.0, rstd[:], OP.mult, OP.mult),
              reads=(bmv, brstd), writes=(bnmr,))
        k.act(xnst[s][:], xt[s][:, 0:c.D], AF.Identity, reads=(bxt[s], bnmr, brstd),
              writes=(bxnst[s],), bias=nmr[:], scale=rstd[:])
        for g in range(0, c.KD, 4):
            n4 = min(4, c.KD - g)
            half = (g // 4) % 2
            lastw = None
            for j in range(n4):
                lastw = k.op("tensor",
                             lambda e, o=pbt[:, (half * 4 + j) * 128:(half * 4 + j + 1) * 128],
                             i=xnst[s][:, (g + j) * 128:(g + j + 1) * 128], d=ident[:]:
                             e.transpose(o, i, d),
                             reads=(bxnst[s], bident), writes=(),
                             xw=(bpb7.writers + bpb7.readers) if j == 0 else ())
            bpb7.writers = [lastw]
            bpb7.readers = []
            dst = xnT[:, 0:c.KD * c.TOKS].rearrange("p (i t) -> p i t", i=c.KD)[
                :, g:g + n4, t * 128:(t + 1) * 128]
            src = pbt.rearrange("p (i t) -> p i t", i=8)[:, half * 4:half * 4 + n4, 0:128]
            wr = (bxnT,) if (t == c.NT - 1 and g + 4 >= c.KD) else ()
            k.act(dst, src, AF.Copy, reads=(bpb7,), writes=wr)
    return xnT, bxnT


def emit_projection(k: MK, acts, bacts, w_ap_fn, d_in, d_out, consume, feature_out,
                    w_is_bf, wq="sync", seed=None, owd=None, wdram=None):
    c = k.c
    KD = d_in // 128
    OWD = owd or min(512, d_out)
    NOG = d_out // OWD
    nw = OWD // 128
    NS = 6
    wb = [k.sb(f"pj_wb{i}", [128, 512], BF16) for i in range(NS)]
    bwb = [k.buf(f"pj_wb{i}") for i in range(NS)]
    swb = [k.sem(f"s_pjwb{'h' if w_is_bf else 'g'}{i}") for i in range(NS)]
    pb = [k.ps(f"pb{i}", [128, 512], F32) for i in range(8)]
    bpb = [k.buf(f"pb{i}") for i in range(8)]
    av = acts[:, 0:KD * c.TOKS].rearrange("p (i t) -> p i t", i=KD)
    ngroups = (nw * c.NTC) if feature_out else c.NT
    assert ngroups <= 8, f"psum groups {ngroups}"
    wr = (wdram,) if wdram is not None else ()
    q2 = wq if w_is_bf else "gpsimd"

    for og in range(NOG):
        for kk in range(KD):
            s = (og * KD + kk) % NS
            k.dma(q2, wb[s][:, 0:OWD], w_ap_fn(kk, og), swb[s], reads=wr,
                  writes=(bwb[s],))
            if feature_out:
                for i4 in range(nw):
                    for tcc in range(c.NTC):
                        g = i4 * c.NTC + tcc
                        k.mm(pb[g][:, 0:c.OW], wb[s][:, i4 * 128:(i4 + 1) * 128],
                             av[:, kk, tcc * c.OW:(tcc + 1) * c.OW],
                             start=(kk == 0), stop=(kk == KD - 1),
                             rbufs=(bwb[s], bacts), wbuf=bpb[g])
            else:
                for jt in range(c.NT):
                    if kk == 0 and seed is not None:
                        sa, sbv, srb = seed(og, jt)
                        k.mm(pb[jt][:, 0:OWD], sa, sbv, start=True, stop=False,
                             rbufs=srb, wbuf=bpb[jt])
                    k.mm(pb[jt][:, 0:OWD], av[:, kk, jt * 128:(jt + 1) * 128],
                         wb[s][:, 0:OWD], start=(kk == 0 and seed is None),
                         stop=(kk == KD - 1), rbufs=(bwb[s], bacts), wbuf=bpb[jt])
        if feature_out:
            for i4 in range(nw):
                for tcc in range(c.NTC):
                    g = i4 * c.NTC + tcc
                    consume(og * nw + i4, tcc, pb[g], bpb[g])
        else:
            for jt in range(c.NT):
                consume(jt, og, pb[jt], bpb[jt])


def emit_ffn_quant(k: MK, inp):
    c = k.c
    nc = k.nc
    R = c.n_cores
    FS = c.DFF // R
    D = c.D
    ident = k.sb("ident_bf", [128, 128], BF16)
    bident = k.buf("ident_bf")
    onesf = k.sb("ones_f32", [128, 1], F32)
    bonesf = k.buf("ones_f32")
    A2q = k.sb("A2", [128, 4 * 4096], BF16)
    A3q = k.sb("A3", [128, 4 * 4096], BF16)
    wtA = A2q[:, 0:2 * max(2048, k.c.D)].bitcast(F32)
    wtB = A3q[:, 0:2 * max(2048, k.c.D)].bitcast(F32)
    bwtA = k.buf("A2")
    bwtB = k.buf("A3")
    sqzA = k.sem("s_qzta")
    sqzB = k.sem("s_qztb")
    acc = k.sb("qz_acc", [128, 2], F32)
    bacc = k.buf("qz_acc")
    col = k.sb("qz_col", [128, 1], F32)
    bcol = k.buf("qz_col")
    scs = k.sb("qz_scs", [1, 2], F32)
    bscs = k.buf("qz_scs")
    thr = k.sb("qz_thr", [128, 4], F32)
    bthr = k.buf("qz_thr")
    qa = k.sb("qz_qa", [128, 2048], BF16)
    bqa = k.buf("qz_qa")
    qb2 = k.sb("qz_qb2", [128, 2048], BF16)
    bqb2 = k.buf("qz_qb2")
    qq = k.sb("qz_qq", [128, 2048], BF16)
    bqq = k.buf("qz_qq")
    pb6 = k.ps("pb6", [128, 512], F32)
    bpb6 = k.buf("pb6")
    pb7 = k.ps("pb7", [128, 512], F32)
    pbt = pb7[:].bitcast(BF16)
    bpb7 = k.buf("pb7")
    PAN1 = k.sb("A2", [128, 4 * 4096], BF16)
    bPAN1 = k.buf("A2")
    PAN2 = k.sb("A3", [128, 4 * 4096], BF16)
    bPAN2 = k.buf("A3")

    w1s, w2s = inp["w1s"], inp["w2s"]
    bw1s, bw2s = k.buf("d_w1s"), k.buf("d_w2s")
    sc_in = nc.dram_tensor("qz_scin", [1, 2], F32).ap()
    sc_out = nc.dram_tensor("qz_scout", [1, 2], F32).ap()
    bscin, bscout = k.buf("d_scin"), k.buf("d_scout")
    w1q_in = nc.dram_tensor("w1q_in", [D, FS], BF16).ap()
    w2q_in = nc.dram_tensor("w2q_in", [FS, D], BF16).ap()
    bw1qin, bw2qin = k.buf("d_w1qin"), k.buf("d_w2qin")
    w1q_g = nc.dram_tensor("w1q_g", [R * D, FS], BF16).ap()
    w2q_g = nc.dram_tensor("w2q_g", [R * FS, D], BF16).ap()
    bw1qg, bw2qg = k.buf("d_w1qg"), k.buf("d_w2qg")
    sdio = k.sem("s_qzio")
    spans = [k.sem("s_qzpan1"), k.sem("s_qzpan2")]

    k.vop("memset", (onesf[:], 1.0), reads=(), writes=(bonesf,))
    cnt = 0
    for wi, (wsrc, bsrc, rows, cols) in enumerate(((w1s, bw1s, FS, D), (w2s, bw2s, D, FS))):
        for r0 in range(0, rows, 128):
            nc2 = min(2048, cols)
            for c0 in range(0, cols, nc2):
                s = cnt % 2
                k.dma("scalar", wtA[:, 0:nc2], wsrc[r0:r0 + 128, c0:c0 + nc2], sqzA,
                      reads=(bsrc,), writes=(bwtA,))
                k.vop("tensor_reduce", (col[:], wtA[:, 0:nc2]), reads=(bwtA,),
                      writes=(bcol,), kw=dict(axis=AX.X, op=OP.add, apply_absolute_value=True))
                if cnt == 0:
                    k.vop("memset", (acc[:], 0.0), reads=(), writes=(bacc,))
                k.vop("scalar_tensor_tensor",
                      (acc[:, wi:wi + 1], col[:], 1.0, acc[:, wi:wi + 1], OP.mult, OP.add),
                      reads=(bcol, bacc), writes=(bacc,))
                cnt += 1
    k.mm(pb6[0:1, 0:2], onesf[:], acc[:], start=True, stop=True,
         rbufs=(bonesf, bacc), wbuf=bpb6)
    k.act(scs[:], pb6[0:1, 0:2], AF.Copy, reads=(bpb6,), writes=(bscs,))
    k.dma("gpsimd", sc_in[:, :], scs[:], sdio, reads=(bscs,), writes=(bscin,))
    k.collective("AllReduce", OP.add, [list(range(R))], sc_in.opt(), sc_out.opt(),
                 bscin, bscout, "scale")
    k.dma("gpsimd", thr[:, 0:2], sc_out[0:1, :].partition_broadcast(128).squeeze(1), sdio,
          reads=(bscout,), writes=(bthr,))
    k.vop("tensor_scalar", (thr[:, 2:4], thr[:, 0:2], -0.7 / (c.DFF * D), None,
                            OP.mult, OP.bypass), reads=(bthr,), writes=())
    k.vop("tensor_scalar", (thr[:, 0:2], thr[:, 0:2], 0.7 / (c.DFF * D), None,
                            OP.mult, OP.bypass), reads=(bthr,), writes=(bthr,))
    for wi, (wsrc, bsrc, rows, cols, pan, bpan, q_in, bq_in, npan_cols) in enumerate((
            (w1s, bw1s, FS, D, PAN1, bPAN1, w1q_in, bw1qin, FS),
            (w2s, bw2s, D, FS, PAN2, bPAN2, w2q_in, bw2qin, D))):
        npan = cols // 128
        wtP = wtB if wi == 0 else wtA
        bwtP = bwtB if wi == 0 else bwtA
        sqzP = sqzB if wi == 0 else sqzA
        for r0 in range(0, rows, 128):
            k.dma("scalar", wtP[:, 0:cols], wsrc[r0:r0 + 128, 0:cols], sqzP,
                  reads=(bsrc,), writes=(bwtP,))
            k.vop("tensor_scalar", (qa[:, 0:cols], wtP[:, 0:cols], thr[:, wi:wi + 1], None,
                                    OP.is_ge, OP.bypass), reads=(bwtP, bthr), writes=(bqa,))
            k.vop("tensor_scalar", (qb2[:, 0:cols], wtP[:, 0:cols], thr[:, 2 + wi:3 + wi],
                                    None, OP.is_le, OP.bypass), reads=(bwtP, bthr), writes=(bqb2,))
            k.vop("tensor_tensor", (qq[:, 0:cols], qa[:, 0:cols], qb2[:, 0:cols], OP.subtract),
                  reads=(bqa, bqb2), writes=(bqq,))
            for g in range(0, npan, 4):
                n4 = min(4, npan - g)
                lastw = None
                for j in range(n4):
                    lastw = k.op("tensor",
                                 lambda e, o=pbt[:, j * 128:(j + 1) * 128],
                                 i=qq[:, (g + j) * 128:(g + j + 1) * 128], d=ident[:]:
                                 e.transpose(o, i, d),
                                 reads=(bqq, bident), writes=(),
                                 xw=(bpb7.writers + bpb7.readers) if j == 0 else ())
                bpb7.writers = [lastw]
                bpb7.readers = []
                dst = pan[:].rearrange("p (i t) -> p i t", i=npan)[:, g:g + n4, r0:r0 + 128]
                src = pbt.rearrange("p (i t) -> p i t", i=8)[:, 0:n4, 0:128]
                wr = (bpan,) if (r0 + 128 >= rows and g + 4 >= npan) else ()
                k.act(dst, src, AF.Copy, reads=(bpb7,), writes=wr)
        sp = spans[wi]
        for p0 in range(npan):
            k.dma("scalar", q_in[p0 * 128:(p0 + 1) * 128, 0:rows],
                  pan[:].rearrange("p (i t) -> p i t", i=npan)[:, p0, 0:rows], sp,
                  reads=(bpan,), writes=(bq_in,) if p0 == npan - 1 else ())
        bq_in.writers = [(sp, sp.n)]
        bq_in.readers = []
    k.collective("AllGather", OP.bypass, [list(range(R))], w1q_in.opt(), w1q_g.opt(),
                 bw1qin, bw1qg, "w1g")
    k.collective("AllGather", OP.bypass, [list(range(R))], w2q_in.opt(), w2q_g.opt(),
                 bw2qin, bw2qg, "w2g")
    return w1q_g, bw1qg, w2q_g, bw2qg


def emit_consts(k: MK, inp):
    c = k.c
    nc = k.nc
    d = {}
    CB = 6 * 128 + (c.NCHO + 1) * 128 + 2 * c.TOKS
    cbf = k.sb("cbf", [128, CB], BF16)
    bcbf = k.buf("cbf")
    scbf = k.sem("s_cbf")
    cf = k.sb("cf32", [128, 132], F32)
    bcf = k.buf("cf32")
    scf = k.sem("s_cf32")
    k.dma("sync", cbf[:], inp["cbf_in"][:, :], scbf, reads=(k.buf("d_cbf"),), writes=(bcbf,))
    k.dma("sync", cf[:], inp["cf32_in"][:, :], scf, reads=(k.buf("d_cf32"),), writes=(bcf,))
    identb = k.sb("ident_bf", [128, 128], BF16)
    bidentb = k.buf("ident_bf")
    k.vop("tensor_copy", (identb[:], cbf[:, 0:128]), reads=(bcbf,), writes=(bidentb,))
    d["maskS"] = cbf[:, 128:256]
    d["maskST"] = cbf[:, 256:384]
    d["incT"] = cbf[:, 384:512]
    d["onesrow"] = cbf[0:1, 512:640]
    d["onescol"] = cbf[:, 640:641]
    d["wcol2"] = cbf[:, 641:642]
    d["rtcol"] = cbf[:, 642:643]
    d["bcbf"] = bcbf
    M0 = 6 * 128
    d["maskrev"] = cbf[:, M0:M0 + (c.NCHO + 1) * 128]
    d["cosT"] = cbf[:, M0 + (c.NCHO + 1) * 128:M0 + (c.NCHO + 1) * 128 + c.TOKS]
    d["sstar"] = cbf[:, M0 + (c.NCHO + 1) * 128 + c.TOKS:M0 + (c.NCHO + 1) * 128 + 2 * c.TOKS]
    d["identf"] = cf[:, 0:128]
    d["sel0"] = cf[:, 128:129]
    d["sel1"] = cf[:, 129:130]
    d["epscol"] = cf[:, 130:131]
    d["onecol"] = cf[:, 131:132]
    d["bcf"] = bcf
    NB1 = c.DFF // 128
    b1t = k.sb("b1t", [NB1, 128], F32)
    bb1t = k.buf("b1t")
    sb1 = k.sem("s_b1")
    k.dma("sync", b1t[:], inp["b1"].rearrange("(a b) -> a b", b=128), sb1,
          reads=(k.buf("d_b1"),), writes=(bb1t,))
    sb2 = k.sem("s_b2")
    sbo = k.sem("s_bo")
    pb6 = k.ps("pb6", [128, 512], F32)
    bpb6 = k.buf("pb6")
    k.tp(pb6[:, 0:NB1], b1t[:], cf[0:NB1, 0:NB1],
         rbufs=(bb1t, bcf), wbuf=bpb6)
    b1col = k.sb("b1col", [128, NB1], F32)
    bb1col = k.buf("b1col")
    k.act(b1col[:], pb6[:, 0:NB1], AF.Copy, reads=(bpb6,), writes=(bb1col,))
    d["b1col"] = b1col
    d["bb1col"] = bb1col
    xt0 = k.sb("ln_xt0", [128, max(2048, c.D)], F32)
    bxt0 = k.buf("ln_xt0")
    brows = k.sb("biasrows", [1, 2 * c.D], BF16)
    bbrows = k.buf("biasrows")
    b2r = xt0[0:1, 0:c.D]
    k.dma("sync", b2r, inp["b2"].rearrange("(o a) -> o a", o=1), sb2,
          reads=(k.buf("d_b2"),), writes=(bxt0,))
    k.vop("tensor_copy", (brows[0:1, 0:c.D], b2r), reads=(bxt0,), writes=())
    bor = xt0[0:1, 0:c.D]
    k.dma("sync", bor, inp["bo"].rearrange("(o a) -> o a", o=1), sbo,
          reads=(k.buf("d_bo"),), writes=(bxt0,))
    k.vop("tensor_copy", (brows[0:1, c.D:2 * c.D], bor), reads=(bxt0,),
          writes=(bbrows,))
    d["b2rb"], d["bb2rb"] = brows[0:1, 0:c.D], bbrows
    d["borb"], d["bborb"] = brows[0:1, c.D:2 * c.D], bbrows
    return d


def _wap(w, OWD):
    return lambda kk, og: w[kk * 128:(kk + 1) * 128, og * OWD:(og + 1) * OWD]


def _store_feature_cb(k: MK, dram_ap, dram_buf, dram_sem, extra=None):
    c = k.c
    stg = [k.sb(f"fst_{i}", [128, 512], BF16) for i in range(2)]
    bstg = [k.buf(f"fst_{i}") for i in range(2)]
    state = {"n": 0}

    def cb(i, tc, bank, bbank):
        s = state["n"] % 2
        state["n"] += 1
        k.act(stg[s][:, 0:c.OW], bank[:, 0:c.OW], AF.Copy, reads=(bbank,), writes=(bstg[s],))
        k.dma("scalar", dram_ap[i * 128:(i + 1) * 128, tc * c.OW:(tc + 1) * c.OW],
              stg[s][:, 0:c.OW], dram_sem, reads=(bstg[s],), writes=(dram_buf,))
        if extra is not None:
            extra(i, tc, bank, bbank, stg[s], bstg[s])
    return cb


def _store_token_cb(k: MK, dram_ap, dram_buf, dram_sem, OWD):
    c = k.c
    stg = [k.sb(f"tst_{i}", [128, 512], BF16) for i in range(2)]
    bstg = [k.buf(f"tst_{i}") for i in range(2)]
    state = {"n": 0}

    def cb(jt, og, bank, bbank):
        s = state["n"] % 2
        state["n"] += 1
        k.act(stg[s][:, 0:OWD], bank[:, 0:OWD], AF.Copy, reads=(bbank,), writes=(bstg[s],))
        k.dma("scalar", dram_ap[jt * 128:(jt + 1) * 128, og * OWD:(og + 1) * OWD],
              stg[s][:, 0:OWD], dram_sem, reads=(bstg[s],), writes=(dram_buf,))
    return cb


def emit_rope(k: MK, arena, barena, nheads, co, cf):
    c = k.c
    M0 = 6 * 128 + (c.NCHO + 1) * 128
    cos0, ss0 = M0, M0 + c.TOKS
    cbf = k.sb("cbf", [128, 6 * 128 + (c.NCHO + 1) * 128 + 2 * c.TOKS], BF16)
    hT1 = k.sb("dl_ksmT", [128, max(c.D, (c.NCHO + 1) * 128, 2 * c.TOKS)], BF16)
    bt1 = k.buf("dl_ksmT")
    hT2 = k.sb("dl_kbT", [128, max(c.D, 2 * c.TOKS)], BF16)
    brot = k.buf("dl_kbT")
    t1 = hT1[0:128, 0:2 * c.TOKS].bitcast(F32)
    rot = hT2[:, 0:c.TOKS]
    for h in range(nheads):
        lo, hi = h * c.TOKS, (h + 1) * c.TOKS
        k.act(hT2[0:64, 0:c.TOKS], arena[64:128, lo:hi], AF.Copy,
              reads=(barena,), writes=())
        k.act(hT2[64:128, 0:c.TOKS], arena[0:64, lo:hi], AF.Copy,
              reads=(barena,), writes=(brot,))
        brot.writers = [(k.es["scalar"], k.es["scalar"].n)]
        brot.readers = []
        k.vop("tensor_tensor", (t1, arena[:, lo:hi],
                                cbf[:, cos0:cos0 + c.TOKS], OP.mult),
              reads=(barena, co["bcbf"]), writes=(bt1,))
        k.vop("tensor_tensor", (rot, rot, cbf[:, ss0:ss0 + c.TOKS], OP.mult),
              reads=(brot, co["bcbf"]), writes=(brot,))
        k.vop("tensor_tensor", (arena[:, lo:hi], t1, rot, OP.add),
              reads=(bt1, brot), writes=(barena,))


def emit_attention(k: MK, co, HQ, kv_of, KH, kT_g, bkT_g, v_g, bv_g, q_dram, bq_dram,
                   q_arena, bq_arena, wcol_ap, out_cb):
    c = k.c
    cbf = k.sb("cbf", [128, 6 * 128 + (c.NCHO + 1) * 128 + 2 * c.TOKS], BF16)
    cf = k.sb("cf32", [128, 132], F32)
    kpan = k.sb("qz_qa", [128, 2048], BF16)
    bkpan = k.buf("qz_qa")
    skp = k.sem("s_atkp")
    vpan = k.sb("qz_qb2", [128, 2048], BF16)
    bvpan = k.buf("qz_qb2")
    svp = k.sem("s_atvp")
    qq2 = k.sb("qz_qq", [128, 2048], BF16)
    qpan = qq2[:, 0:c.TOKS]
    bqpan = k.buf("at_qpan")
    bqpan.writers = list(k.buf("qz_qq").writers) + k.buf("qz_qq").readers
    sqp = k.sem("s_atqp")
    hSsfE = k.sb("dl_Ssf", [128, max(c.H * 128, 2 * c.TOKS)], F32)
    expb = [qq2[:, c.TOKS:2 * c.TOKS],
            hSsfE[:, c.TOKS:c.TOKS + c.TOKS // 2].bitcast(BF16)]
    bexpb = [k.buf("at_expb"), k.buf("at_expb2")]
    bexpb[0].writers = list(k.buf("qz_qq").writers) + k.buf("qz_qq").readers
    bexpb[1].writers = list(k.buf("dl_Ssf").writers) + k.buf("dl_Ssf").readers
    MW = (c.NCHO + 1) * 128
    hmsk0 = k.sb("dl_ksmT", [128, max(c.D, (c.NCHO + 1) * 128, 2 * c.TOKS)], BF16)
    hmsk1 = k.sb("dl_kbT", [128, max(c.D, 2 * c.TOKS)], BF16)
    msk = [hmsk0[:, 0:MW], hmsk1[:, 0:MW]]
    bmsk = [k.buf("at_msk0"), k.buf("at_msk1")]
    bmsk[0].writers = list(k.buf("dl_ksmT").writers) + k.buf("dl_ksmT").readers
    bmsk[1].writers = list(k.buf("dl_kbT").writers) + k.buf("dl_kbT").readers
    hSsf = k.sb("dl_Ssf", [128, max(c.H * 128, 2 * c.TOKS)], F32)
    rn = hSsf[0:1, 0:c.TOKS]
    brn = k.buf("at_rn")
    brn.writers = list(k.buf("dl_Ssf").writers) + k.buf("dl_Ssf").readers
    rnb = hSsf[0:1, c.TOKS + c.TOKS // 2:2 * c.TOKS].bitcast(BF16)
    brnb = k.buf("at_rnb")
    brnb.writers = list(brn.writers)
    hSb = k.sb("dl_Sb", [128, max(c.H * 128, c.TOKS + 512)], BF16)
    rb = hSb[:, 0:c.TOKS]
    brb = k.buf("at_rb")
    brb.writers = list(k.buf("dl_Sb").writers) + k.buf("dl_Sb").readers
    pb = [k.ps(f"pb{i}", [128, 512], F32) for i in range(8)]
    bpb = [k.buf(f"pb{i}") for i in range(8)]
    NSEG = (c.TOKS + 511) // 512
    scale = 1.0 / math.sqrt(128.0)
    SB = {0: (2, 3), 1: (6, 7)}

    for h in range(HQ):
        hkv = kv_of(h)
        if (h == 0) or (kv_of(h - 1) != hkv):
            last = None
            for sec in range(2):
                last = k.dma("sync", kpan[:, sec * c.TOKS:(sec + 1) * c.TOKS],
                             kT_g[sec * KH * 128 + hkv * 128:sec * KH * 128 + (hkv + 1) * 128, :],
                             skp, reads=(bkT_g,),
                             xw=(bkpan.writers + bkpan.readers) if sec == 0 else ())
            bkpan.writers = [last]
            bkpan.readers = []
            vv = v_g.rearrange("(a p) d -> p a d", p=128)
            nq = max(1, c.NCH // 4)
            for qd in range(0, c.NCH, nq):
                last = k.dma("sync",
                             vpan[:, 0:c.NCH * 128].rearrange("p (a e) -> p a e", a=c.NCH)[:, qd:qd + nq, :],
                             vv[:, qd:qd + nq, hkv * 128:(hkv + 1) * 128],
                             svp, reads=(bv_g,),
                             xw=(bvpan.writers + bvpan.readers) if qd == 0 else ())
            bvpan.writers = [last]
            bvpan.readers = []
        if q_dram is not None:
            k.dma("sync", qpan, q_dram[h * 128:(h + 1) * 128, :], sqp,
                  reads=(bq_dram,), writes=(bqpan,))

            def qv(a, b):
                return qpan[0:128, a:b]
            bqv = bqpan
        else:
            def qv(a, b, h=h):
                return q_arena[:, h * c.TOKS + a:h * c.TOKS + b]
            bqv = bq_arena

        def geom(kb):
            jlo = max(0, kb - c.NCHO)
            return jlo, jlo * 128, min(c.NT - 1, kb), (min(c.NT - 1, kb) + 1) * 128

        def do_score(kb):
            par = kb % 2
            b0, b1 = SB[par]
            jlo, qlo, jhi, mend = geom(kb)
            e0 = 6 * 128 + (c.NCHO - kb + jlo) * 128
            first_m = True
            for s in range(NSEG):
                s0, s1 = s * 512, min((s + 1) * 512, c.TOKS)
                if s1 <= qlo:
                    continue
                a0 = max(s0, qlo)
                bank = pb[b0 + s]
                bbank = bpb[b0 + s]
                k.mm(bank[:, a0 - s0:s1 - s0], kpan[:, kb * 128:(kb + 1) * 128],
                     qv(a0, s1), start=True, stop=True, rbufs=(bkpan, bqv), wbuf=bbank)
                m1 = min(s1, mend)
                if a0 < m1:
                    k.vop("tensor_tensor",
                          (msk[par][:, a0 - qlo:m1 - qlo], bank[:, a0 - s0:m1 - s0],
                           cbf[:, e0 + (a0 - qlo):e0 + (m1 - qlo)], OP.add),
                          reads=(bbank, co["bcbf"]),
                          writes=(bmsk[par],) if m1 == mend else (),
                          xw=(bmsk[par].writers + bmsk[par].readers) if first_m else ())
                    first_m = False
                    if m1 == mend:
                        bmsk[par].writers = [(k.es["vector"], k.es["vector"].n)]
                        bmsk[par].readers = []

        def do_pv(kb):
            par = kb % 2
            b0, b1 = SB[par]
            jlo, qlo, jhi, mend = geom(kb)
            first_e = True
            for s in range(NSEG):
                s0, s1 = s * 512, min((s + 1) * 512, c.TOKS)
                if s1 <= qlo:
                    continue
                a0 = max(s0, qlo)
                m1 = min(s1, mend)
                xw = (bexpb[par].writers + bexpb[par].readers) if first_e else ()
                if a0 < m1:
                    k.act(expb[par][:, a0:m1], msk[par][:, a0 - qlo:m1 - qlo], AF.Exp,
                          reads=(bmsk[par],), writes=(), scale=scale, xw=xw)
                    first_e = False
                    xw = ()
                if m1 < s1:
                    k.act(expb[par][:, max(a0, m1):s1],
                          pb[b0 + s][:, max(a0, m1) - s0:s1 - s0], AF.Exp,
                          reads=(bpb[b0 + s],), writes=(), scale=scale, xw=xw)
                    first_e = False
            bexpb[par].writers = [(k.es["scalar"], k.es["scalar"].n)]
            bexpb[par].readers = []
            for s in range(NSEG):
                s0, s1 = s * 512, min((s + 1) * 512, c.TOKS)
                if s1 <= qlo:
                    continue
                a0 = max(s0, qlo)
                lastkb = min(c.NCH - 1, c.NCHO + s1 // 128 - 1)
                k.mm(pb[s][:, a0 - s0:s1 - s0], vpan[:, kb * 128:(kb + 1) * 128],
                     expb[par][:, a0:s1], start=(kb == 0), stop=(kb == lastkb),
                     rbufs=(bvpan, bexpb[par]), wbuf=bpb[s])
                k.mm(pb[4 + s][0:1, a0 - s0:s1 - s0], wcol_ap, expb[par][:, a0:s1],
                     start=(kb == 0), stop=(kb == lastkb),
                     rbufs=(co["bcbf"], bexpb[par]), wbuf=bpb[4 + s])

        do_score(0)
        for kb in range(c.NCH):
            if kb + 1 < c.NCH:
                do_score(kb + 1)
            do_pv(kb)
        for s in range(NSEG):
            s0, s1 = s * 512, min((s + 1) * 512, c.TOKS)
            k.vop("reciprocal", (rn[0:1, s0:s1], pb[4 + s][0:1, 0:s1 - s0]),
                  reads=(bpb[4 + s],), writes=(),
                  xw=(brn.writers + brn.readers) if s == 0 else ())
        brn.writers = [(k.es["vector"], k.es["vector"].n)]
        brn.readers = []
        k.vop("tensor_copy", (rnb, rn), reads=(brn,), writes=(brnb,))
        for s in range(NSEG):
            s0, s1 = s * 512, min((s + 1) * 512, c.TOKS)
            k.mm(pb[6][:, 0:s1 - s0], cbf[0:1, 512:640], rnb[0:1, s0:s1],
                 start=True, stop=True, rbufs=(co["bcbf"], brnb), wbuf=bpb[6])
            k.act(rb[:, s0:s1], pb[6][:, 0:s1 - s0], AF.Copy, reads=(bpb[6],),
                  writes=(), xw=(brb.writers + brb.readers) if s == 0 else ())
        brb.writers = [(k.es["scalar"], k.es["scalar"].n)]
        brb.readers = []
        out_cb(h, pb, bpb, rb, brb)


def emit_delta(k: MK, co, qsm, bqsm, kfeat_g, bkfeat_g, ktok_g, bktok_g, vtok_g, bvtok_g,
               ocD, bocD):
    c = k.c
    H, W, D = c.H, c.H * 128, c.D
    cbf = k.sb("cbf", [128, 6 * 128 + (c.NCHO + 1) * 128 + 2 * c.TOKS], BF16)
    cf = k.sb("cf32", [128, 132], F32)
    identb = k.sb("ident_bf", [128, 128], BF16)
    bident = k.buf("ident_bf")
    sel0 = cf[:, 128:129]
    sel1 = cf[:, 129:130]
    maskS, maskST, incT = cbf[:, 128:256], cbf[:, 256:384], cbf[:, 384:512]
    onescol = cbf[:, 640:641]
    onesrow = cbf[0:1, 512:640]

    AXA = k.sb("A_xnT", [128, max(c.KD * c.TOKS, 8 * c.D)], BF16)
    bxnT = k.buf("A_xnT")

    NREG = max(c.KD * c.TOKS, 8 * c.D) // c.D
    DB = 1

    def region(i, name):
        bb = k.buf(name)
        bb.writers = list(bxnT.writers) + list(bxnT.readers)
        bb.readers = []
        return AXA[:, i * c.D:(i + 1) * c.D], bb

    regs = {}
    for par in range(DB):
        for idx, nm in enumerate(("kF", "kT", "vT", "eFT", "gC", "gFb", "ksmF", "kbF")):
            regs[(nm, par)] = region(idx + 8 * par, f"dlc_{nm}{par}")
    skFs = [k.sem("s_dlkF0"), k.sem("s_dlkF1")]
    skTs = [k.sem("s_dlkT0"), k.sem("s_dlkT1")]
    svTs = [k.sem("s_dlvT0"), k.sem("s_dlvT1")]
    gA = k.sb("qz_qa", [128, 2048], BF16)
    bgA = k.buf("qz_qa")
    gB = k.sb("qz_qb2", [128, 2048], BF16)
    bgB = k.buf("qz_qb2")
    tmpF = k.sb("dl_tmpF", [128, 512], F32)
    btmpF = k.buf("dl_tmpF")
    rnF = k.sb("dl_rnF", [1, 512], F32)
    brnF = k.buf("dl_rnF")
    rnFb = k.sb("dl_rnFb", [1, 512], BF16)
    brnFb = k.buf("dl_rnFb")
    rT = k.sb("dl_rT", [128, max(H, 2)], F32)
    brT = k.buf("dl_rT")
    rTi = k.sb("dl_rTi", [128, max(H, 2)], F32)
    brTi = k.buf("dl_rTi")
    ksmT = k.sb("dl_ksmT", [128, max(D, (c.NCHO + 1) * 128, 2 * c.TOKS)], BF16)
    bksmT = k.buf("dl_ksmT")
    kbT = k.sb("dl_kbT", [128, max(D, 2 * c.TOKS)], BF16)
    bkbT = k.buf("dl_kbT")
    NG = min(4, H)
    GW = NG * 128
    Lm = k.sb("dl_Lm", [128, GW], BF16)
    bLm = k.buf("dl_Lm")
    LTm = k.sb("dl_LTm", [128, GW], BF16)
    bLTm = k.buf("dl_LTm")
    XT0 = k.sb("dl_XT0", [128, GW], BF16)
    bXT0 = k.buf("dl_XT0")
    XT1 = k.sb("dl_ocs", [128, GW], BF16)
    bXT1 = k.buf("dl_ocs")
    P1b = k.sb("dl_P1", [128, GW], BF16)
    bP1 = k.buf("dl_P1")
    PT1b = k.sb("dl_PT1", [128, GW], BF16)
    bPT1 = k.buf("dl_PT1")
    P2b = k.sb("dl_P2", [128, GW], BF16)
    bP2 = k.buf("dl_P2")
    PT2b = k.sb("dl_PT2", [128, GW], BF16)
    bPT2 = k.buf("dl_PT2")
    XT2, bXT2 = P1b, bP1
    XT3, bXT3 = PT1b, bPT1
    wN = PT1b
    bwN = bPT1
    t4 = P1b
    bt4 = bP1
    ub = k.sb("dl_ub", [128, GW], BF16)
    bub = k.buf("dl_ub")
    atb = k.sb("dl_atb", [128, GW], BF16)
    batb = k.buf("dl_atb")
    to4 = k.sb("dl_to4", [128, GW], BF16)
    bto4 = k.buf("dl_to4")
    ocs, bocs = XT1, bXT1
    Ssf = k.sb("dl_Ssf", [128, max(W, 2 * c.TOKS)], F32)
    bSsf = k.buf("dl_Ssf")
    Sb = k.sb("dl_Sb", [128, max(W, c.TOKS + 512)], BF16)
    bSb = k.buf("dl_Sb")
    pb = [k.ps(f"pb{i}", [128, 512], F32) for i in range(8)]
    bpb = [k.buf(f"pb{i}") for i in range(8)]

    def q4(i):
        return slice(i * 128, (i + 1) * 128)

    k.vop("memset", (Ssf[:], 0.0), reads=(), writes=(bSsf,))
    k.vop("memset", (Sb[:], 0.0), reads=(), writes=(bSb,))

    kf_r = kfeat_g.rearrange("(a h p) t -> a p h t", a=2, p=128)
    EPS = c.EPS_SM
    for ch in range(c.NCH):
        par = ch % DB
        (dkF, bkF) = regs[("kF", par)]
        (dkT, bkT2) = regs[("kT", par)]
        (dvT, bvT2) = regs[("vT", par)]
        (eF, beF) = regs[("eFT", par)]
        (gC, bgC) = regs[("gC", par)]
        (gFb, bgFb) = regs[("gFb", par)]
        (ksmF, bksmF) = regs[("ksmF", par)]
        (kbF, bkbF) = regs[("kbF", par)]
        eT, beT = eF, beF
        gTb, bgTb = gFb, bgFb
        sec, ccol = ch // c.NCHO, ch % c.NCHO
        j = ch % c.NCHO
        ng4 = max(1, H // 4)
        last = None
        for g4 in range(0, H, ng4):
            last = k.dma("sync",
                         dkF.rearrange("p (h e) -> p h e", h=H)[:, g4:g4 + ng4, 0:128],
                         kf_r[sec, :, g4:g4 + ng4, ccol * 128:(ccol + 1) * 128],
                         skFs[par], reads=(bkfeat_g,),
                         xw=(bkF.writers + bkF.readers) if g4 == 0 else ())
        bkF.writers = [last]
        bkF.readers = []
        k.dma("sync", dkT, ktok_g[ch * 128:(ch + 1) * 128, :], skTs[par],
              reads=(bktok_g,), writes=(bkT2,))
        k.dma("sync", dvT, vtok_g[ch * 128:(ch + 1) * 128, :], svTs[par],
              reads=(bvtok_g,), writes=(bvT2,))
        k.act(eF, dkF, AF.Exp, reads=(bkF,), writes=(beF,))
        for sg in range(0, W, 512):
            sw = min(512, W - sg)
            k.mm(pb[6][0:1, 0:sw], onescol, eF[:, sg:sg + sw], start=True, stop=True,
                 rbufs=(co["bcbf"], beF), wbuf=bpb[6])
            k.vop("reciprocal", (rnF[0:1, 0:sw], pb[6][0:1, 0:sw]),
                  reads=(bpb[6],), writes=(brnF,))
            k.vop("tensor_copy", (rnFb[0:1, 0:sw], rnF[0:1, 0:sw]), reads=(brnF,),
                  writes=(brnFb,))
            k.mm(pb[6][:, 0:sw], onesrow, rnFb[0:1, 0:sw], start=True, stop=True,
                 rbufs=(co["bcbf"], brnFb), wbuf=bpb[6])
            k.vop("tensor_tensor", (tmpF[:, 0:sw], eF[:, sg:sg + sw], pb[6][:, 0:sw],
                                    OP.mult), reads=(beF, bpb[6]), writes=(btmpF,))
            k.vop("tensor_scalar", (ksmF[:, sg:sg + sw], tmpF[:, 0:sw], EPS, 1.0 - EPS,
                                    OP.max, OP.min), reads=(btmpF,),
                  writes=(),
                  xw=(bksmF.writers + bksmF.readers) if sg == 0 else ())
        bksmF.writers = [(k.es["vector"], k.es["vector"].n)]
        bksmF.readers = []
        k.act(gA[:, 0:W], dkF, AF.Abs, reads=(bkF,), writes=(bgA,))
        k.act(gB[:, 0:W], gA[:, 0:W], AF.Exp, reads=(bgA,), writes=(bgB,), scale=-1.0)
        k.act(gA[:, 0:W], gB[:, 0:W], AF.Ln, reads=(bgB, co["bcf"]), writes=(bgA,),
              bias=co["onecol"])
        k.vop("tensor_scalar", (gB[:, 0:W], dkF, 0.0, 1.0 / 16.0, OP.min, OP.mult),
              reads=(bkF,), writes=(bgB,))
        k.vop("scalar_tensor_tensor", (gC, gA[:, 0:W], -1.0 / 16.0, gB[:, 0:W],
                                       OP.mult, OP.add), reads=(bgA, bgB), writes=(bgC,))
        k.vop("tensor_scalar", (gFb, gC, -16.0, 16.0, OP.max, OP.min),
              reads=(bgC,), writes=(bgFb,))
        k.vop("tensor_tensor", (kbF, ksmF, gFb, OP.mult),
              reads=(bksmF, bgFb), writes=(bkbF,))
        k.act(eT, dkT, AF.Exp, reads=(bkT2,), writes=(beT,))
        k.vop("tensor_reduce", (rT[:, 0:H], eT.rearrange("p (h e) -> p h e", h=H)),
              reads=(beT,), writes=(brT,), kw=dict(axis=AX.X, op=OP.add))
        k.vop("reciprocal", (rTi[:, 0:H], rT[:, 0:H]), reads=(brT,), writes=(brTi,))
        for h in range(H):
            k.vop("tensor_scalar", (gC[:, h * 128:(h + 1) * 128],
                                    eT[:, h * 128:(h + 1) * 128], rTi[:, h:h + 1], None,
                                    OP.mult, OP.bypass), reads=(beT, brTi),
                  writes=(),
                  xw=(bgC.writers + bgC.readers) if h == 0 else ())
        bgC.writers = [(k.es["vector"], k.es["vector"].n)]
        bgC.readers = []
        k.vop("tensor_scalar", (ksmT[:, 0:D], gC, EPS, 1.0 - EPS, OP.max, OP.min),
              reads=(bgC,), writes=(bksmT,))
        k.act(gA[:, 0:D], dkT, AF.Abs, reads=(bkT2,), writes=(bgA,))
        k.act(gB[:, 0:D], gA[:, 0:D], AF.Exp, reads=(bgA,), writes=(bgB,), scale=-1.0)
        k.act(gA[:, 0:D], gB[:, 0:D], AF.Ln, reads=(bgB, co["bcf"]), writes=(bgA,),
              bias=co["onecol"])
        k.vop("tensor_scalar", (gB[:, 0:D], dkT, 0.0, 1.0 / 16.0, OP.min, OP.mult),
              reads=(bkT2,), writes=(bgB,))
        k.vop("scalar_tensor_tensor", (gC, gA[:, 0:D], -1.0 / 16.0, gB[:, 0:D],
                                       OP.mult, OP.add), reads=(bgA, bgB), writes=(bgC,))
        k.vop("tensor_scalar", (gTb, gC, -16.0, 16.0, OP.max, OP.min),
              reads=(bgC,), writes=(bgTb,))
        k.vop("tensor_tensor", (kbT[:, 0:D], ksmT[:, 0:D], gTb, OP.mult),
              reads=(bksmT, bgTb), writes=(bkbT,))
        k.vop("tensor_tensor", (dvT, dvT, gTb, OP.mult),
              reads=(bgTb,), writes=(bvT2,))
        for g0 in range(0, H, NG):
            gs = slice(g0 * 128, (g0 + NG) * 128)

            def hs(i, g0=g0):
                return slice((g0 + i) * 128, (g0 + i + 1) * 128)

            for i in range(NG):
                k.mm(pb[2][:, q4(i)], kbF[:, hs(i)], ksmF[:, hs(i)], start=True,
                     stop=True, rbufs=(bkbF, bksmF),
                     wbuf=bpb[2] if i == NG - 1 else k.buf("_nw"),
                     xw=(bpb[2].writers + bpb[2].readers) if i == 0 else ())
                k.mm(pb[3][:, q4(i)], ksmF[:, hs(i)], kbF[:, hs(i)], start=True,
                     stop=True, rbufs=(bksmF, bkbF),
                     wbuf=bpb[3] if i == NG - 1 else k.buf("_nw"),
                     xw=(bpb[3].writers + bpb[3].readers) if i == 0 else ())
            for i in range(NG):
                h = g0 + i
                k.mm(pb[6][:, q4(i)], Sb[:, hs(i)],
                     qsm[:, h * c.TOKS + j * 128:h * c.TOKS + (j + 1) * 128],
                     start=True, stop=True, rbufs=(bSb, bqsm),
                     wbuf=bpb[6] if i == NG - 1 else k.buf("_nw"),
                     xw=(bpb[6].writers + bpb[6].readers) if i == 0 else ())
            selc0 = sel0 if ch < c.NCHO else sel1
            k.act(to4[:], pb[6][:, 0:GW], AF.Copy, reads=(bpb[6], co["bcf"]),
                  writes=(bto4,), scale=selc0)
            for i in range(NG):
                h = g0 + i
                k.mm(pb[0][:, q4(i)], ksmF[:, hs(i)],
                     qsm[:, h * c.TOKS + j * 128:h * c.TOKS + (j + 1) * 128],
                     start=True, stop=True, rbufs=(bksmF, bqsm),
                     wbuf=bpb[0] if i == NG - 1 else k.buf("_nw"),
                     xw=(bpb[0].writers + bpb[0].readers) if i == 0 else ())
            k.vop("tensor_tensor", (atb[:].rearrange("p (a e) -> p a e", a=NG),
                                    pb[0][:, 0:GW].rearrange("p (a e) -> p a e", a=NG),
                                    incT.unsqueeze(1).to_broadcast([128, NG, 128]),
                                    OP.mult), reads=(bpb[0], co["bcbf"]),
                  writes=(batb,))
            mS3 = maskS.unsqueeze(1).to_broadcast([128, NG, 128])
            mST3 = maskST.unsqueeze(1).to_broadcast([128, NG, 128])
            id3 = identb[:].unsqueeze(1).to_broadcast([128, NG, 128])
            inc3 = incT.unsqueeze(1).to_broadcast([128, NG, 128])
            k.vop("tensor_tensor", (Lm[:].rearrange("p (a e) -> p a e", a=NG),
                                    pb[2][:, 0:GW].rearrange("p (a e) -> p a e", a=NG),
                                    mS3, OP.mult), reads=(bpb[2], co["bcbf"]),
                  writes=(bLm,))
            k.vop("tensor_tensor", (LTm[:].rearrange("p (a e) -> p a e", a=NG),
                                    pb[3][:, 0:GW].rearrange("p (a e) -> p a e", a=NG),
                                    mST3, OP.mult), reads=(bpb[3], co["bcbf"]),
                  writes=(bLTm,))
            k.vop("tensor_tensor", (XT0[:].rearrange("p (a e) -> p a e", a=NG), id3,
                                    LTm[:].rearrange("p (a e) -> p a e", a=NG),
                                    OP.subtract), reads=(bident, bLTm), writes=(bXT0,))
            for i in range(NG):
                k.mm(pb[4][:, q4(i)], LTm[:, q4(i)], Lm[:, q4(i)], start=True, stop=True,
                     rbufs=(bLTm, bLm), wbuf=bpb[4] if i == NG - 1 else k.buf("_nw"),
                     xw=(bpb[4].writers + bpb[4].readers) if i == 0 else ())
                k.mm(pb[5][:, q4(i)], Lm[:, q4(i)], LTm[:, q4(i)], start=True, stop=True,
                     rbufs=(bLm, bLTm), wbuf=bpb[5] if i == NG - 1 else k.buf("_nw"),
                     xw=(bpb[5].writers + bpb[5].readers) if i == 0 else ())
            k.act(P1b[:], pb[4][:, 0:GW], AF.Copy, reads=(bpb[4],), writes=(bP1,))
            k.act(PT1b[:], pb[5][:, 0:GW], AF.Copy, reads=(bpb[5],), writes=(bPT1,))
            for i in range(NG):
                k.mm(pb[6][:, q4(i)], P1b[:, q4(i)], XT0[:, q4(i)], start=True, stop=True,
                     rbufs=(bP1, bXT0), wbuf=bpb[6] if i == NG - 1 else k.buf("_nw"),
                     xw=(bpb[6].writers + bpb[6].readers) if i == 0 else ())
            k.vop("tensor_tensor", (XT1[:], pb[6][:, 0:GW], XT0[:], OP.add),
                  reads=(bpb[6], bXT0), writes=(bXT1,))
            for i in range(NG):
                k.mm(pb[4][:, q4(i)], PT1b[:, q4(i)], P1b[:, q4(i)], start=True, stop=True,
                     rbufs=(bPT1, bP1), wbuf=bpb[4] if i == NG - 1 else k.buf("_nw"),
                     xw=(bpb[4].writers + bpb[4].readers) if i == 0 else ())
                k.mm(pb[5][:, q4(i)], P1b[:, q4(i)], PT1b[:, q4(i)], start=True, stop=True,
                     rbufs=(bP1, bPT1), wbuf=bpb[5] if i == NG - 1 else k.buf("_nw"),
                     xw=(bpb[5].writers + bpb[5].readers) if i == 0 else ())
            k.act(P2b[:], pb[4][:, 0:GW], AF.Copy, reads=(bpb[4],), writes=(bP2,))
            k.act(PT2b[:], pb[5][:, 0:GW], AF.Copy, reads=(bpb[5],), writes=(bPT2,))
            for i in range(NG):
                k.mm(pb[6][:, q4(i)], P2b[:, q4(i)], XT1[:, q4(i)], start=True, stop=True,
                     rbufs=(bP2, bXT1), wbuf=bpb[6] if i == NG - 1 else k.buf("_nw"),
                     xw=(bpb[6].writers + bpb[6].readers) if i == 0 else ())
            k.vop("tensor_tensor", (XT2[:], pb[6][:, 0:GW], XT1[:], OP.add),
                  reads=(bpb[6], bXT1), writes=(bXT2,))
            XTf, bXTf = XT2, bXT2
            for i in range(NG):
                k.mm(pb[7][:, q4(i)], kbT[:, hs(i)], XTf[:, q4(i)], start=True, stop=True,
                     rbufs=(bkbT, bXTf), wbuf=bpb[7] if i == NG - 1 else k.buf("_nw"),
                     xw=(bpb[7].writers + bpb[7].readers) if i == 0 else ())
            k.act(wN[:], pb[7][:, 0:GW], AF.Copy, reads=(bpb[7],), writes=(bwN,),
                  scale=-1.0)
            for i in range(NG):
                k.mm(pb[6][:, q4(i)], XTf[:, q4(i)], dvT[:, hs(i)], start=True, stop=True,
                     rbufs=(bXTf, bvT2), wbuf=bpb[6] if i == NG - 1 else k.buf("_nw"),
                     xw=(bpb[6].writers + bpb[6].readers) if i == 0 else ())
                k.mm(pb[7][:, q4(i)], wN[:, q4(i)], Sb[:, hs(i)], start=True, stop=True,
                     rbufs=(bwN, bSb), wbuf=bpb[7] if i == NG - 1 else k.buf("_nw"),
                     xw=(bpb[7].writers + bpb[7].readers) if i == 0 else ())
            k.act(t4[:], pb[6][:, 0:GW], AF.Copy, reads=(bpb[6],), writes=(bt4,))
            k.vop("tensor_tensor", (ub[:], pb[7][:, 0:GW], t4[:], OP.add),
                  reads=(bpb[7], bt4), writes=(bub,))
            for i in range(NG):
                k.mm(pb[7][:, q4(i)], ub[:, q4(i)], atb[:, q4(i)], start=True, stop=True,
                     rbufs=(bub, batb), wbuf=bpb[7] if i == NG - 1 else k.buf("_nw"),
                     xw=(bpb[7].writers + bpb[7].readers) if i == 0 else ())
            selc = selc0
            oc3 = ocD[:, 0:H * c.TOKS].rearrange("p (h t) -> p h t", h=H)[
                :, g0:g0 + NG, j * 128:(j + 1) * 128]
            if ch < c.NCHO:
                k.vop("scalar_tensor_tensor",
                      (oc3, pb[7][:, 0:GW].rearrange("p (a e) -> p a e", a=NG), selc,
                       to4[:].rearrange("p (a e) -> p a e", a=NG), OP.mult, OP.add),
                      reads=(bpb[7], bto4, co["bcf"]), writes=(bocD,))
            else:
                k.vop("scalar_tensor_tensor",
                      (ocs[:].rearrange("p (a e) -> p a e", a=NG),
                       pb[7][:, 0:GW].rearrange("p (a e) -> p a e", a=NG), selc,
                       to4[:].rearrange("p (a e) -> p a e", a=NG), OP.mult, OP.add),
                      reads=(bpb[7], bto4, co["bcf"]), writes=(bocs,))
                k.vop("tensor_tensor",
                      (oc3, ocs[:].rearrange("p (a e) -> p a e", a=NG), oc3, OP.add),
                      reads=(bocs,), writes=(bocD,))
            for i in range(NG):
                k.mm(pb[3][:, q4(i)], ksmT[:, hs(i)], ub[:, q4(i)], start=True, stop=True,
                     rbufs=(bksmT, bub), wbuf=bpb[3] if i == NG - 1 else k.buf("_nw"),
                     xw=(bpb[3].writers + bpb[3].readers) if i == 0 else ())
            k.vop("tensor_tensor", (Ssf[:, gs], pb[3][:, 0:GW], Ssf[:, gs], OP.add),
                  reads=(bpb[3],), writes=(bSsf,))
            k.vop("tensor_copy", (Sb[:, gs], Ssf[:, gs]), reads=(bSsf,), writes=(bSb,))
    for (nm, par2), (_, bb) in regs.items():
        bxnT.readers = bxnT.readers + bb.writers + bb.readers


def emit_residual_store(k: MK, xprev, bxprev, xout, bxout, sxout, OWD):
    c = k.c
    xpc = [k.sb(f"rs_xp{i}", [128, 512], F32) for i in range(2)]
    bxpc = [k.buf(f"rs_xp{i}") for i in range(2)]
    sxp = [k.sem(f"s_rsxp{i}") for i in range(2)]
    stg = [k.sb(f"rs_st{i}", [128, 512], F32) for i in range(2)]
    bstg = [k.buf(f"rs_st{i}") for i in range(2)]
    xoc = [k.sb("rs_xo0", [128, 512], F32)] * 2
    bxoc = [k.buf("rs_xo0")] * 2
    state = {"n": 0}

    def cb(jt, og, bank, bbank):
        s = state["n"] % 2
        state["n"] += 1
        k.dma("sync", xpc[s][:, 0:OWD], xprev[jt * 128:(jt + 1) * 128,
                                              og * OWD:(og + 1) * OWD], sxp[s],
              reads=(bxprev,), writes=(bxpc[s],))
        k.act(stg[s][:, 0:OWD], bank[:, 0:OWD], AF.Copy, reads=(bbank,),
              writes=(bstg[s],))
        k.vop("tensor_tensor", (xoc[s][:, 0:OWD], stg[s][:, 0:OWD], xpc[s][:, 0:OWD],
                                OP.add), reads=(bstg[s], bxpc[s]), writes=(bxoc[s],))
        k.dma("scalar", xout[jt * 128:(jt + 1) * 128, og * OWD:(og + 1) * OWD],
              xoc[s][:, 0:OWD], sxout, reads=(bxoc[s],), writes=(bxout,))
    return cb


def stage_attn_a(k: MK, co, inp, xprev, bxprev, xout, bxout, sxout):
    c = k.c
    nc = k.nc
    PAIRS = [[2 * i, 2 * i + 1] for i in range(c.n_cores // 2)]
    xnT, bxnT = emit_ln_xnT(k, co, xprev, bxprev)
    kfeat_in = nc.dram_tensor("kfeat_in", [c.D, c.TOKS], BF16).ap()
    bkfi = k.buf("d_kfeat_in")
    kfeat_g = nc.dram_tensor("kfeat_g", [2 * c.D, c.TOKS], BF16).ap()
    bkfg = k.buf("d_kfeat_g")
    ktok_in = nc.dram_tensor("ktok_in", [c.TOKS, c.D], BF16).ap()
    bkti = k.buf("d_ktok_in")
    ktok_g = nc.dram_tensor("ktok_g", [c.SEQ, c.D], BF16).ap()
    bktg = k.buf("d_ktok_g")
    vtok_in = nc.dram_tensor("vtok_in", [c.TOKS, c.D], BF16).ap()
    bvti = k.buf("d_vtok_in")
    vtok_g = nc.dram_tensor("vtok_g", [c.SEQ, c.D], BF16).ap()
    bvtg = k.buf("d_vtok_g")
    qraw_d = nc.dram_tensor("qraw_d", [c.D, c.TOKS], BF16).ap()
    bqrd = k.buf("d_qraw_d")
    sst_kf = k.sem("s_st_kf")
    sst_kt = k.sem("s_st_kt")
    sst_vt = k.sem("s_st_vt")
    sst_qr = k.sem("s_st_qr")

    wk = inp["wk_a"]
    bwk = k.buf("d_wk_a")
    emit_projection(k, xnT, bxnT, _wap(wk, min(512, c.D)),
                    c.D, c.D, _store_feature_cb(k, kfeat_in, bkfi, sst_kf), True, False)
    k.collective("AllGather", OP.bypass, PAIRS, kfeat_in.opt(), kfeat_g.opt(), bkfi, bkfg,
                 "kfa")
    emit_projection(k, xnT, bxnT, _wap(wk, min(512, c.D)),
                    c.D, c.D, _store_token_cb(k, ktok_in, bkti, sst_kt, min(512, c.D)),
                    False, False)
    k.collective("AllGather", OP.bypass, PAIRS, ktok_in.opt(), ktok_g.opt(), bkti, bktg,
                 "kta")
    wv = inp["wv_a"]
    emit_projection(k, xnT, bxnT, _wap(wv, min(512, c.D)),
                    c.D, c.D, _store_token_cb(k, vtok_in, bvti, sst_vt, min(512, c.D)),
                    False, False)
    k.collective("AllGather", OP.bypass, PAIRS, vtok_in.opt(), vtok_g.opt(), bvti, bvtg,
                 "vta")
    A2 = k.sb("A2", [128, 4 * 4096], BF16)
    bA2 = k.buf("A2")
    qsm = A2
    cbf = k.sb("cbf", [128, 6 * 128 + (c.NCHO + 1) * 128 + 2 * c.TOKS], BF16)
    rtcol = cbf[:, 642:643]
    onesrow = cbf[0:1, 512:640]
    EPSL = c.EPS_SM / math.sqrt(128.0)
    EPSH = (1.0 - c.EPS_SM) / math.sqrt(128.0)
    wq = inp["wq_a"]
    emit_projection(k, xnT, bxnT, _wap(wq, min(512, c.D)), c.D, c.D,
                    _store_feature_cb(k, qraw_d, bqrd, sst_qr), True, False)
    qld = [k.sb("qs_ld0", [128, 512], BF16)] * 2
    bqld = [k.buf("qs_ld0")] * 2
    sqld = [k.sem("s_qsld0")] * 2
    ebuf = k.sb("qa_e", [128, 512], BF16)
    bebuf = k.buf("qa_e")
    rq = k.sb("dl_rnF", [1, 512], F32)
    brq = k.buf("dl_rnF")
    rqb = k.sb("dl_rnFb", [1, 512], BF16)
    brqb = k.buf("dl_rnFb")
    tq = k.sb("dl_tmpF", [128, 512], F32)
    btq = k.buf("dl_tmpF")
    pb = [k.ps(f"pb{i}", [128, 512], F32) for i in range(8)]
    bpb = [k.buf(f"pb{i}") for i in range(8)]
    nstep = 0
    for i in range(c.H):
        for tc in range(c.NTC):
            s = nstep % 2
            nstep += 1
            k.dma("sync", qld[s][:, 0:c.OW],
                  qraw_d[i * 128:(i + 1) * 128, tc * c.OW:(tc + 1) * c.OW], sqld[s],
                  reads=(bqrd,), writes=(bqld[s],))
            k.act(ebuf[:, 0:c.OW], qld[s][:, 0:c.OW], AF.Exp, reads=(bqld[s],),
                  writes=(bebuf,))
            g2 = nstep % 2
            k.mm(pb[g2][0:1, 0:c.OW], rtcol, ebuf[:, 0:c.OW], start=True, stop=True,
                 rbufs=(co["bcbf"], bebuf), wbuf=bpb[g2])
            k.vop("reciprocal", (rq[0:1, 0:c.OW], pb[g2][0:1, 0:c.OW]), reads=(bpb[g2],),
                  writes=(brq,))
            k.vop("tensor_copy", (rqb[0:1, 0:c.OW], rq[0:1, 0:c.OW]), reads=(brq,),
                  writes=(brqb,))
            k.mm(pb[g2][:, 0:c.OW], onesrow, rqb[0:1, 0:c.OW], start=True, stop=True,
                 rbufs=(co["bcbf"], brqb), wbuf=bpb[g2])
            k.vop("tensor_tensor", (tq[:, 0:c.OW], ebuf[:, 0:c.OW], pb[g2][:, 0:c.OW],
                                    OP.mult), reads=(bebuf, bpb[g2]), writes=(btq,))
            k.vop("tensor_scalar", (qsm[:, i * c.TOKS + tc * c.OW:i * c.TOKS +
                                        (tc + 1) * c.OW],
                                    tq[:, 0:c.OW], EPSL, EPSH, OP.max, OP.min),
                  reads=(btq,), writes=(bA2,))
    A3 = k.sb("A3", [128, 4 * 4096], BF16)
    bA3 = k.buf("A3")
    emit_delta(k, co, qsm, bA2, kfeat_g, bkfg, ktok_g, bktg, vtok_g, bvtg, A3, bA3)
    fin = k.sb("A_xnT", [128, max(c.KD * c.TOKS, 8 * c.D)], BF16)
    bfin = k.buf("A_xnT")
    hSb2 = k.sb("dl_Sb", [128, max(c.H * 128, c.TOKS + 512)], BF16)
    ots = hSb2[:, c.TOKS:c.TOKS + 512]
    bots = k.buf("aa_ots")
    bots.writers = list(k.buf("dl_Sb").writers) + k.buf("dl_Sb").readers
    cbf2 = k.sb("cbf", [128, 6 * 128], BF16)
    wcol2 = cbf2[:, 641:642]

    def out_cb(h, pb, bpb, rbv, brbv):
        for s2 in range((c.TOKS + 511) // 512):
            s0, s1 = s2 * 512, min((s2 + 1) * 512, c.TOKS)
            k.vop("tensor_tensor", (ots[:, 0:s1 - s0], pb[s2][:, 0:s1 - s0],
                                    rbv[:, s0:s1], OP.mult),
                  reads=(bpb[s2], brbv), writes=(bots,))
            k.vop("tensor_tensor", (fin[:, h * c.TOKS + s0:h * c.TOKS + s1],
                                    ots[:, 0:s1 - s0],
                                    A3[:, h * c.TOKS + s0:h * c.TOKS + s1], OP.add),
                  reads=(bots, bA3), writes=(bfin,))

    emit_attention(k, co, c.H, lambda h: h, c.H, kfeat_g, bkfg, vtok_g, bvtg,
                   qraw_d, bqrd, None, None, wcol2, out_cb)
    wo = inp["wo_a"]
    emit_projection(k, fin, bfin, _wap(wo, min(512, c.D)), c.D, c.D,
                    emit_residual_store(k, xprev, bxprev, xout, bxout, sxout,
                                        min(512, c.D)), False, False)


def stage_ffn(k: MK, co, inp, w1qg, bw1qg, w2qg, bw2qg, xprev, bxprev, xout, bxout, sxout,
              acc0, bacc0, acc1, bacc1):
    c = k.c
    nc = k.nc
    R = c.n_cores
    FS = c.DFF // R
    xnT, bxnT = emit_ln_xnT(k, co, xprev, bxprev)
    slab_ar = [k.sb("A2", [128, 4 * 4096], BF16), k.sb("A3", [128, 4 * 4096], BF16)]
    bslab = [k.buf("A2"), k.buf("A3")]
    b2rb = co["b2rb"]
    cbf = k.sb("cbf", [128, 6 * 128], BF16)
    onesrow = cbf[0:1, 512:640]
    KS = c.SLAB // 128
    w1r = w1qg.rearrange("(r d) f -> r d f", r=R)

    for sl in range(c.NSLAB):
        ar, bar = slab_ar[sl % 2], bslab[sl % 2]

        def h_cb(i, tc, bank, bbank, sl=sl, ar=ar, bar=bar):
            fglob = sl * KS + i
            k.act(ar[:, i * c.TOKS + tc * c.OW:i * c.TOKS + (tc + 1) * c.OW],
                  bank[:, 0:c.OW], AF.Gelu, reads=(bbank, co["bb1col"]), writes=(bar,),
                  bias=co["b1col"][:, fglob:fglob + 1])

        OWD1 = min(512, FS)

        def w1_ap(kk, og, sl=sl):
            fglob0 = sl * c.SLAB + og * OWD1
            r = fglob0 // FS
            floc = fglob0 - r * FS
            return w1r[r, kk * 128:(kk + 1) * 128, floc:floc + OWD1]

        emit_projection(k, xnT, bxnT, w1_ap, c.D, c.SLAB, h_cb, True, True,
                        owd=min(512, FS), wdram=bw1qg)

        xp = xprev if sl == 0 else (acc0 if sl % 2 == 1 else acc1)
        bxp = bxprev if sl == 0 else (bacc0 if sl % 2 == 1 else bacc1)
        xo = xout if sl == c.NSLAB - 1 else (acc0 if sl % 2 == 0 else acc1)
        bxo = bxout if sl == c.NSLAB - 1 else (bacc0 if sl % 2 == 0 else bacc1)
        seed = None
        if sl == 0:
            def seed(og, jt, OWD2=min(512, c.D)):
                return (onesrow, b2rb[0:1, og * OWD2:(og + 1) * OWD2],
                        (co["bcbf"], co["bb2rb"]))

        OWD2 = min(512, c.D)

        def w2_ap(kk, og, sl=sl):
            return w2qg[sl * c.SLAB + kk * 128:sl * c.SLAB + (kk + 1) * 128,
                        og * OWD2:(og + 1) * OWD2]

        emit_projection(k, ar, bar, w2_ap, c.SLAB, c.D,
                        emit_residual_store(k, xp, bxp, xo, bxo,
                                            sxout if sl == c.NSLAB - 1 else k.sem("s_acc"),
                                            OWD2),
                        False, True, wdram=bw2qg)


def stage_attn_b(k: MK, co, inp, xprev, bxprev, xout, bxout, sxout):
    c = k.c
    nc = k.nc
    PAIRS = [[2 * i, 2 * i + 1] for i in range(c.n_cores // 2)]
    cf = k.sb("cf32", [128, 132], F32)
    cbf = k.sb("cbf", [128, 6 * 128 + (c.NCHO + 1) * 128 + 2 * c.TOKS], BF16)
    xnT, bxnT = emit_ln_xnT(k, co, xprev, bxprev)
    kb_in = nc.dram_tensor("kb_in", [c.DKV, c.TOKS], BF16).ap()
    bkbi = k.buf("d_kb_in")
    kb_g = nc.dram_tensor("kb_g", [2 * c.DKV, c.TOKS], BF16).ap()
    bkbg = k.buf("d_kb_g")
    vb_in = nc.dram_tensor("vb_in", [c.TOKS, c.DKV], BF16).ap()
    bvbi = k.buf("d_vb_in")
    vb_g = nc.dram_tensor("vb_g", [c.SEQ, c.DKV], BF16).ap()
    bvbg = k.buf("d_vb_g")
    sst_kb = k.sem("s_st_kb")
    sst_vb = k.sem("s_st_vb")
    M0 = 6 * 128 + (c.NCHO + 1) * 128
    cos0, ss0 = M0, M0 + c.TOKS

    rk1 = k.sb("dl_tmpF", [128, 512], F32)
    brk1 = k.buf("dl_tmpF")
    rk2 = k.sb("rs_st0", [128, 512], F32)
    brk2 = k.buf("rs_st0")
    kstg = [k.sb(f"fst_{i}", [128, 512], BF16) for i in range(2)]
    bkstg = [k.buf(f"fst_{i}") for i in range(2)]
    kst = {"n": 0}

    kev = k.sb("qa_e", [128, 512], BF16)
    bkev = k.buf("qa_e")

    def k_cb(i, tc, bank, bbank):
        s = kst["n"] % 2
        kst["n"] += 1
        t0 = tc * c.OW
        k.act(kev[:, 0:c.OW], bank[:, 0:c.OW], AF.Copy, reads=(bbank,), writes=(bkev,))
        k.act(rk2[0:64, 0:c.OW], kev[64:128, 0:c.OW], AF.Copy, reads=(bkev,), writes=())
        k.act(rk2[64:128, 0:c.OW], kev[0:64, 0:c.OW], AF.Copy,
              reads=(bkev,), writes=(brk2,))
        k.vop("tensor_tensor", (rk2[:, 0:c.OW], rk2[:, 0:c.OW],
                                cbf[:, ss0 + t0:ss0 + t0 + c.OW], OP.mult),
              reads=(co["bcbf"],), writes=(brk2,))
        k.vop("tensor_tensor", (rk1[:, 0:c.OW], kev[:, 0:c.OW],
                                cbf[:, cos0 + t0:cos0 + t0 + c.OW], OP.mult),
              reads=(bkev, co["bcbf"]), writes=(brk1,))
        k.vop("tensor_tensor", (kstg[s][:, 0:c.OW], rk1[:, 0:c.OW], rk2[:, 0:c.OW], OP.add),
              reads=(brk1, brk2), writes=(bkstg[s],))
        k.dma("scalar", kb_in[i * 128:(i + 1) * 128, t0:t0 + c.OW], kstg[s][:, 0:c.OW],
              sst_kb, reads=(bkstg[s],), writes=(bkbi,))

    wkb = inp["wk_b"]
    emit_projection(k, xnT, bxnT, _wap(wkb, min(512, c.DKV)), c.D, c.DKV, k_cb,
                    True, False)
    k.collective("AllGather", OP.bypass, PAIRS, kb_in.opt(), kb_g.opt(), bkbi, bkbg, "kb")
    wvb = inp["wv_b"]
    emit_projection(k, xnT, bxnT, _wap(wvb, min(512, c.DKV)), c.D, c.DKV,
                    _store_token_cb(k, vb_in, bvbi, sst_vb, min(512, c.DKV)),
                    False, False)
    k.collective("AllGather", OP.bypass, PAIRS, vb_in.opt(), vb_g.opt(), bvbi, bvbg, "vb")
    A2 = k.sb("A2", [128, 4 * 4096], BF16)
    bA2 = k.buf("A2")
    qst = {"n": 0}

    def q_cb(i, tc, bank, bbank):
        k.act(A2[:, i * c.TOKS + tc * c.OW:i * c.TOKS + (tc + 1) * c.OW], bank[:, 0:c.OW],
              AF.Copy, reads=(bbank,), writes=(bA2,))

    wqb = inp["wq_b"]
    emit_projection(k, xnT, bxnT, _wap(wqb, min(512, c.D)), c.D, c.D, q_cb, True, False)
    emit_rope(k, A2, bA2, c.H, co, cf)
    A3 = k.sb("A3", [128, 4 * 4096], BF16)
    bA3 = k.buf("A3")
    onescol = cbf[:, 640:641]

    def out_cb(h, pb, bpb, rbv, brbv):
        for s2 in range((c.TOKS + 511) // 512):
            s0, s1 = s2 * 512, min((s2 + 1) * 512, c.TOKS)
            k.vop("tensor_tensor", (A3[:, h * c.TOKS + s0:h * c.TOKS + s1],
                                    pb[s2][:, 0:s1 - s0], rbv[:, s0:s1], OP.mult),
                  reads=(bpb[s2], brbv), writes=(bA3,))

    reps = c.H // c.KVB
    emit_attention(k, co, c.H, lambda h: h // reps, c.KVB, kb_g, bkbg, vb_g, bvbg,
                   None, None, A2, bA2, onescol, out_cb)
    onesrow = cbf[0:1, 512:640]
    borb = co["borb"]

    def seed(og, jt, OWD2=min(512, c.D)):
        return (onesrow, borb[0:1, og * OWD2:(og + 1) * OWD2], (co["bcbf"], co["bborb"]))

    wob = inp["wo_b"]
    emit_projection(k, A3, bA3, _wap(wob, min(512, c.D)), c.D, c.D,
                    emit_residual_store(k, xprev, bxprev, xout, bxout, sxout,
                                        min(512, c.D)), False, False, seed=seed)


def build_program(cfg: Cfg, want):
    k = MK(cfg)
    nc = k.nc
    c = cfg
    inp = {}
    inp["xs"] = nc.dram_tensor("xs", [c.TOKS, c.D], F32, kind="ExternalInput").ap()
    for nm, sh in (("wq_a", [c.D, c.D]), ("wk_a", [c.D, c.D]), ("wv_a", [c.D, c.D]),
                   ("wo_a", [c.D, c.D]), ("wq_b", [c.D, c.D]), ("wk_b", [c.D, c.DKV]),
                   ("wv_b", [c.D, c.DKV]), ("wo_b", [c.D, c.D])):
        inp[nm] = nc.dram_tensor(nm, sh, F32, kind="ExternalInput").ap()
    inp["w1s"] = nc.dram_tensor("w1s", [c.DFF // c.n_cores, c.D], F32,
                                kind="ExternalInput").ap()
    inp["w2s"] = nc.dram_tensor("w2s", [c.D, c.DFF // c.n_cores], F32,
                                kind="ExternalInput").ap()
    inp["b1"] = nc.dram_tensor("b1", [c.DFF], F32, kind="ExternalInput").ap()
    inp["b2"] = nc.dram_tensor("b2", [c.D], F32, kind="ExternalInput").ap()
    inp["bo"] = nc.dram_tensor("bo", [c.D], F32, kind="ExternalInput").ap()
    inp["cbf_in"] = nc.dram_tensor("cbf_in",
                                   [128, 6 * 128 + (c.NCHO + 1) * 128 + 2 * c.TOKS],
                                   BF16, kind="ExternalInput").ap()
    inp["cf32_in"] = nc.dram_tensor("cf32_in", [128, 132], F32,
                                    kind="ExternalInput").ap()
    out = nc.dram_tensor("out_x", [c.TOKS, c.D], F32, kind="ExternalOutput").ap()
    bout = k.buf("d_out")
    bxs = k.buf("d_xs")
    xd = [nc.dram_tensor(f"xd{i}", [c.TOKS, c.D], F32).ap() for i in range(3)]
    bxd = [k.buf(f"d_xd{i}") for i in range(3)]
    acc0 = nc.dram_tensor("acc0", [c.TOKS, c.D], F32).ap()
    bacc0 = k.buf("d_acc0")
    acc1 = nc.dram_tensor("acc1", [c.TOKS, c.D], F32).ap()
    bacc1 = k.buf("d_acc1")
    sxout = k.sem("s_xout")

    co = emit_consts(k, inp)
    if "f1" in want or "f2" in want:
        w1qg, bw1qg, w2qg, bw2qg = emit_ffn_quant(k, inp)

    chain = [(inp["xs"], bxs)]
    for st in want:
        nxt = (out, bout) if st == want[-1] else (xd[len(chain) - 1], bxd[len(chain) - 1])
        xp, bxp = chain[-1]
        if st == "a":
            stage_attn_a(k, co, inp, xp, bxp, nxt[0], nxt[1], sxout)
        elif st in ("f1", "f2"):
            stage_ffn(k, co, inp, w1qg, bw1qg, w2qg, bw2qg, xp, bxp, nxt[0], nxt[1],
                      sxout, acc0, bacc0, acc1, bacc1)
        elif st == "b":
            stage_attn_b(k, co, inp, xp, bxp, nxt[0], nxt[1], sxout)
        chain.append(nxt)
    k.barrier_wait("gpsimd", [(sxout, sxout.n)] + bout.writers)
    return k



def make_core_inputs(cfg: Cfg, core, x_full, wq_a, wk_a, wv_a, wo_a, wq_b, wk_b, wv_b,
                     wo_b, bo, w1, b1, w2, b2):
    c = cfg
    b, h = core // 2, core % 2
    d = {}
    d["xs"] = np.ascontiguousarray(x_full[b, h * c.TOKS:(h + 1) * c.TOKS, :], np.float32)
    d["wq_a"], d["wk_a"], d["wv_a"], d["wo_a"] = wq_a, wk_a, wv_a, wo_a
    d["wq_b"], d["wk_b"], d["wv_b"], d["wo_b"] = wq_b, wk_b, wv_b, wo_b
    FS = c.DFF // c.n_cores
    d["w1s"] = np.ascontiguousarray(w1[core * FS:(core + 1) * FS, :], np.float32)
    d["w2s"] = np.ascontiguousarray(w2[:, core * FS:(core + 1) * FS], np.float32)
    d["b1"], d["b2"], d["bo"] = b1, b2, bo
    cbf = np.zeros((128, 6 * 128 + (c.NCHO + 1) * 128 + 2 * c.TOKS), np.float32)
    cbf[:, 0:128] = np.eye(128)
    jj, ii = np.meshgrid(np.arange(128), np.arange(128), indexing="xy")
    cbf[:, 128:256] = (np.arange(128)[None, :] < np.arange(128)[:, None])
    cbf[:, 256:384] = (np.arange(128)[None, :] > np.arange(128)[:, None])
    cbf[:, 384:512] = (np.arange(128)[:, None] <= np.arange(128)[None, :])
    cbf[0, 512:640] = 1.0
    cbf[:, 640] = 1.0
    cbf[:, 641] = 2.0
    cbf[:, 642] = math.sqrt(128.0)
    M0 = 6 * 128
    for e in range(c.NCHO + 1):
        dd = c.NCHO - e
        rel = dd - c.NCHO * h
        if rel < 0:
            m = np.zeros((128, 128), np.float32)
        elif rel == 0:
            m = np.where(np.arange(128)[:, None] <= np.arange(128)[None, :], 0.0, NEG)
        else:
            m = np.full((128, 128), NEG, np.float32)
        cbf[:, M0 + e * 128:M0 + (e + 1) * 128] = m
    pos = (h * c.TOKS + np.arange(c.TOKS)).astype(np.float64)
    inv = 1.0 / (10000.0 ** (np.arange(0, 128, 2, dtype=np.float64) / 128.0))
    fr = np.outer(pos, inv)
    cosT = np.concatenate([np.cos(fr).T, np.cos(fr).T], 0).astype(np.float32)
    sstar = np.concatenate([-np.sin(fr).T, np.sin(fr).T], 0).astype(np.float32)
    Mc = M0 + (c.NCHO + 1) * 128
    cbf[:, Mc:Mc + c.TOKS] = cosT
    cbf[:, Mc + c.TOKS:Mc + 2 * c.TOKS] = sstar
    d["cbf_in"] = cbf.astype(BF16NP)
    cf = np.zeros((128, 132), np.float32)
    cf[:, 0:128] = np.eye(128)
    cf[:, 128] = 0.5 * (1 - h)
    cf[:, 129] = 0.5 * h
    cf[:, 130] = 1e-5
    cf[:, 131] = 1.0
    d["cf32_in"] = cf
    return d


def run_cfg(cfg: Cfg, in_maps, sim=False):
    k = build_program(cfg, cfg.stages)
    nc = k.build()
    if sim:
        from concourse.bass_interp import MultiCoreSim
        s = MultiCoreSim(nc, num_cores=cfg.n_cores, num_workers=1)
        for i in range(cfg.n_cores):
            for kk, v in in_maps[i].items():
                s.cores[i].tensor(kk)[:] = v
        s.simulate(check_with_hw=False)
        return [{"out_x": np.asarray(s.cores[i].mem_tensor("out_x"))}
                for i in range(cfg.n_cores)]
    from concourse.bass_utils import run_bass_kernel_spmd
    res = run_bass_kernel_spmd(nc, in_maps, core_ids=list(range(cfg.n_cores)))
    return res.results


def kernel(L0_attn_a_x, L0_attn_a_Wq, L0_attn_a_Wk, L0_attn_a_Wv, L0_attn_a_Wo,
           L0_attn_b_Wq, L0_attn_b_Wk, L0_attn_b_Wv, L0_attn_b_Wo, L0_attn_b_bo,
           L0_ffn_w1, L0_ffn_b1, L0_ffn_w2, L0_ffn_b2):
    cfg = Cfg()
    x = np.asarray(L0_attn_a_x, np.float32)
    args = (x, np.asarray(L0_attn_a_Wq, np.float32), np.asarray(L0_attn_a_Wk, np.float32),
            np.asarray(L0_attn_a_Wv, np.float32), np.asarray(L0_attn_a_Wo, np.float32),
            np.asarray(L0_attn_b_Wq, np.float32), np.asarray(L0_attn_b_Wk, np.float32),
            np.asarray(L0_attn_b_Wv, np.float32), np.asarray(L0_attn_b_Wo, np.float32),
            np.asarray(L0_attn_b_bo, np.float32), np.asarray(L0_ffn_w1, np.float32),
            np.asarray(L0_ffn_b1, np.float32), np.asarray(L0_ffn_w2, np.float32),
            np.asarray(L0_ffn_b2, np.float32))
    in_maps = [make_core_inputs(cfg, c, args[0], *args[1:10],
                                args[10], args[11], args[12], args[13])
               for c in range(cfg.n_cores)]
    results = run_cfg(cfg, in_maps, sim=False)
    out = np.empty((4, 2048, 2048), np.float32)
    for c in range(cfg.n_cores):
        b, h = c // 2, c % 2
        out[b, h * cfg.TOKS:(h + 1) * cfg.TOKS, :] = results[c]["out_x"]
    return out


# version 11
# speedup vs baseline: 1.0122x; 1.0122x over previous
import math
import numpy as np
import ml_dtypes
import concourse.bass as bass
import concourse.mybir as mybir
from concourse.alu_op_type import AluOpType
from contextlib import ExitStack

F32 = mybir.dt.float32
BF16 = mybir.dt.bfloat16
AF = mybir.ActivationFunctionType
AX = mybir.AxisListType
OP = AluOpType
BF16NP = ml_dtypes.bfloat16

NEG = -30000.0


class Cfg:
    def __init__(self, D=2048, TOKS=1024, DFF=8192, KVB=4, stages=("a", "f1", "b", "f2"),
                 n_cores=8):
        self.D = D
        self.H = D // 128
        self.DH = 128
        self.TOKS = TOKS
        self.SEQ = 2 * TOKS
        self.NT = TOKS // 128
        self.NCH = self.SEQ // 128
        self.NCHO = TOKS // 128
        self.DFF = DFF
        self.KVB = KVB
        self.DKV = KVB * 128
        self.KD = D // 128
        self.stages = stages
        self.n_cores = n_cores
        self.NLEV = 3
        self.EPS_SM = 1e-6
        self.LN_EPS = 1e-5
        self.SLAB = min(2048, DFF)
        self.NSLAB = DFF // self.SLAB
        self.OW = min(512, TOKS)
        self.NTC = TOKS // self.OW


class Sem:
    __slots__ = ("h", "n", "name")

    def __init__(self, h, name):
        self.h = h
        self.n = 0
        self.name = name

    def inc(self, k=1):
        self.n += k
        return self.n


class Buf:
    __slots__ = ("name", "writers", "readers")

    def __init__(self, name):
        self.name = name
        self.writers = []
        self.readers = []


def _merge(waits):
    d = {}
    for s, v in waits:
        if v > d.get(s, -1):
            d[s] = v
    return tuple(d.items())


class MK:
    ENGS = ("sync", "vector", "scalar", "tensor", "gpsimd")

    def __init__(self, cfg: Cfg):
        self.c = cfg
        self.nc = bass.Bass("TRN2", target_bir_lowering=False)
        self.ops = {e: [] for e in self.ENGS}
        self.ctx = ExitStack()
        self._sb = {}
        self._sems = {}
        self.es = {e: self.sem("es_" + e) for e in ("vector", "scalar", "tensor", "gpsimd")}
        self.bufs = {}
        self.wm = {e: {} for e in self.ENGS}

    def sem(self, name) -> Sem:
        if name not in self._sems:
            h = self.ctx.enter_context(self.nc.semaphore(name))
            self._sems[name] = Sem(h, name)
        return self._sems[name]

    def sb(self, name, shape, dtype):
        if name not in self._sb:
            self._sb[name] = self.ctx.enter_context(self.nc.sbuf_tensor(name, shape, dtype))
        return self._sb[name]

    def ps(self, name, shape, dtype=F32):
        if name not in self._sb:
            self._sb[name] = self.ctx.enter_context(self.nc.psum_tensor(name, shape, dtype))
        return self._sb[name]

    def buf(self, name) -> Buf:
        if name not in self.bufs:
            self.bufs[name] = Buf(name)
        return self.bufs[name]

    def _emit(self, eng, fn, sem, inc, waits, default_inc=False):
        waits = _merge(waits)
        wm = self.wm[eng]
        waits = tuple((s, v) for s, v in waits if v > wm.get(s, -1))
        for s, v in waits:
            wm[s] = v

        def run(e, fn=fn, waits=waits, sem=sem, inc=inc, default_inc=default_inc):
            for s, v in waits:
                e.wait_ge(s.h, v)
            inst = fn(e)
            if inst is not None and sem is not None:
                if default_inc:
                    inst.then_inc(sem.h)
                else:
                    inst.then_inc(sem.h, inc)
        self.ops[eng].append(run)

    def op(self, eng, fn, reads=(), writes=(), xw=(), sem=None, inc=1, default_inc=False,
           no_track_w=False):
        if sem is None:
            sem = self.es[eng]
        waits = list(xw)
        for b in reads:
            waits += b.writers
        for b in writes:
            waits += b.writers + b.readers
        n = sem.inc(inc)
        self._emit(eng, fn, sem, inc, waits, default_inc)
        for b in reads:
            b.readers.append((sem, n))
        if not no_track_w:
            for b in writes:
                b.writers = [(sem, n)]
                b.readers = []
        return sem, n

    def dma(self, q, out, in_, sem, reads=(), writes=(), xw=()):
        return self.op(q, lambda e, o=out, i=in_: e.dma_start(out=o, in_=i),
                       reads=reads, writes=writes, xw=xw, sem=sem, inc=16)

    def mm(self, out_ap, a_ap, b_ap, start, stop, rbufs, wbuf, xw=()):
        if start:
            waits = list(xw) + wbuf.writers + wbuf.readers
        else:
            waits = list(xw)
        for b in rbufs:
            waits += b.writers
        sem = self.es["tensor"]
        n = sem.inc(1)
        self._emit("tensor",
                   lambda e, o=out_ap, a=a_ap, bb=b_ap, st=start, sp=stop:
                   e.matmul(o, a, bb, start=st, stop=sp),
                   sem, 1, waits)
        for b in rbufs:
            b.readers.append((sem, n))
        if stop:
            wbuf.writers = [(sem, n)]
            wbuf.readers = []
        return sem, n

    def tp(self, out_ap, in_ap, ident_ap, rbufs, wbuf, xw=()):
        return self.op("tensor", lambda e, o=out_ap, i=in_ap, d=ident_ap: e.transpose(o, i, d),
                       reads=rbufs, writes=(wbuf,), xw=xw)

    def act(self, out_ap, in_ap, func, reads, writes, bias=None, scale=None, accum=None, xw=()):
        kw = {}
        if bias is not None:
            kw["bias"] = bias
        if scale is not None:
            kw["scale"] = scale
        if accum is not None:
            kw["accum_out"] = accum
        return self.op("scalar", lambda e, o=out_ap, i=in_ap, k=kw: e.activation(o, i, func, **k),
                       reads=reads, writes=writes, xw=xw)

    def vop(self, name, args, reads, writes, xw=(), kw=None):
        kw = kw or {}
        return self.op("vector", lambda e, a=args, k=kw: getattr(e, name)(*a, **k),
                       reads=reads, writes=writes, xw=xw)

    def collective(self, kind, aop, groups, in_ap, out_ap, rbuf, wbuf, name):
        s = self.sem("cc_" + name)
        return self.op("gpsimd",
                       lambda e, k=kind, o=aop, g=groups, i=in_ap, u=out_ap:
                       e.collective_compute(k, o, replica_groups=g, ins=[i], outs=[u]),
                       reads=(rbuf,), writes=(wbuf,), sem=s, inc=1, default_inc=True)

    def barrier_wait(self, eng, pairs):
        self._emit(eng, lambda e: None, None, 0, _merge(pairs))

    def build(self):
        nc = self.nc
        block = self.ctx.enter_context(nc.Block())
        for eng in self.ENGS:
            lst = self.ops[eng]

            def make(lst):
                def body(e):
                    for f in lst:
                        f(e)
                return body
            getattr(block, eng)(make(lst))
        self.ctx.close()
        return nc



def emit_ln_xnT(k: MK, co, x_dram_ap, xd_buf):
    c = k.c
    xnT = k.sb("A_xnT", [128, max(c.KD * c.TOKS, 8 * c.D)], BF16)
    bxnT = k.buf("A_xnT")
    ident = k.sb("ident_bf", [128, 128], BF16)
    bident = k.buf("ident_bf")
    xt = [k.sb("ln_xt0", [128, max(2048, c.D)], F32),
          k.sb("ln_xt1", [128, max(2048, c.D)], F32)]
    bxt = [k.buf("ln_xt0"), k.buf("ln_xt1")]
    sxt = [k.sem("s_lnx0"), k.sem("s_lnx1")]
    xnst = [k.sb("ln_st0", [128, c.D], BF16)] * 2
    bxnst = [k.buf("ln_st0")] * 2
    nseg = (c.D + 511) // 512
    stats = k.sb("ln_stats", [128, 6 * nseg], F32)
    bstats = k.buf("ln_stats")
    mv = k.sb("ln_mv", [128, 2], F32)
    bmv = k.buf("ln_mv")
    srt = k.sb("ln_srt", [128, 1], F32)
    bsrt = k.buf("ln_srt")
    rstd = k.sb("ln_rstd", [128, 1], F32)
    brstd = k.buf("ln_rstd")
    nmr = k.sb("ln_nmr", [128, 1], F32)
    bnmr = k.buf("ln_nmr")
    pb7 = k.ps("pb7", [128, 512], F32)
    pbt = pb7[:].bitcast(BF16)
    bpb7 = k.buf("pb7")

    vsem, tsem = k.es["vector"], k.es["tensor"]
    for t in range(c.NT):
        s = t % 2
        k.dma("sync", xt[s][:, 0:c.D], x_dram_ap[t * 128:(t + 1) * 128, :], sxt[s],
              reads=(xd_buf,), writes=(bxt[s],))
        last = None
        for g in range(nseg):
            g1 = min(512 * (g + 1), c.D)
            last = k.vop("bn_stats", (stats[:, 6 * g:6 * (g + 1)],
                                      xt[s][:, 512 * g:g1]),
                         reads=(bxt[s],), writes=(), xw=(bstats.writers + bstats.readers) if g == 0 else ())
        bstats.writers = [last]
        bstats.readers = []
        k.vop("bn_aggr", (mv[:], stats[:]), reads=(bstats,), writes=(bmv,))
        k.act(srt[:], mv[:, 1:2], AF.Sqrt, reads=(bmv, co["bcf"]), writes=(bsrt,),
              bias=co["epscol"])
        k.vop("reciprocal", (rstd[:], srt[:]), reads=(bsrt,), writes=(brstd,))
        k.vop("scalar_tensor_tensor", (nmr[:], mv[:, 0:1], -1.0, rstd[:], OP.mult, OP.mult),
              reads=(bmv, brstd), writes=(bnmr,))
        k.act(xnst[s][:], xt[s][:, 0:c.D], AF.Identity, reads=(bxt[s], bnmr, brstd),
              writes=(bxnst[s],), bias=nmr[:], scale=rstd[:])
        for g in range(0, c.KD, 4):
            n4 = min(4, c.KD - g)
            half = (g // 4) % 2
            lastw = None
            for j in range(n4):
                lastw = k.op("tensor",
                             lambda e, o=pbt[:, (half * 4 + j) * 128:(half * 4 + j + 1) * 128],
                             i=xnst[s][:, (g + j) * 128:(g + j + 1) * 128], d=ident[:]:
                             e.transpose(o, i, d),
                             reads=(bxnst[s], bident), writes=(),
                             xw=(bpb7.writers + bpb7.readers) if j == 0 else ())
            bpb7.writers = [lastw]
            bpb7.readers = []
            dst = xnT[:, 0:c.KD * c.TOKS].rearrange("p (i t) -> p i t", i=c.KD)[
                :, g:g + n4, t * 128:(t + 1) * 128]
            src = pbt.rearrange("p (i t) -> p i t", i=8)[:, half * 4:half * 4 + n4, 0:128]
            wr = (bxnT,) if (t == c.NT - 1 and g + 4 >= c.KD) else ()
            k.act(dst, src, AF.Copy, reads=(bpb7,), writes=wr)
    return xnT, bxnT


def emit_projection(k: MK, acts, bacts, w_ap_fn, d_in, d_out, consume, feature_out,
                    w_is_bf, wq="sync", seed=None, owd=None, wdram=None):
    c = k.c
    KD = d_in // 128
    OWD = owd or min(512, d_out)
    NOG = d_out // OWD
    nw = OWD // 128
    NS = 8
    wb = [k.sb(f"pj_wb{i}", [128, 512], BF16) for i in range(NS)]
    bwb = [k.buf(f"pj_wb{i}") for i in range(NS)]
    swb = [k.sem(f"s_pjwb{'h' if w_is_bf else 'g'}{i}") for i in range(NS)]
    pb = [k.ps(f"pb{i}", [128, 512], F32) for i in range(8)]
    bpb = [k.buf(f"pb{i}") for i in range(8)]
    av = acts[:, 0:KD * c.TOKS].rearrange("p (i t) -> p i t", i=KD)
    ngroups = (nw * c.NTC) if feature_out else c.NT
    assert ngroups <= 8, f"psum groups {ngroups}"
    wr = (wdram,) if wdram is not None else ()
    q2 = wq if w_is_bf else "gpsimd"

    for og in range(NOG):
        for kk in range(KD):
            s = (og * KD + kk) % NS
            k.dma(q2, wb[s][:, 0:OWD], w_ap_fn(kk, og), swb[s], reads=wr,
                  writes=(bwb[s],))
            if feature_out:
                for i4 in range(nw):
                    for tcc in range(c.NTC):
                        g = i4 * c.NTC + tcc
                        k.mm(pb[g][:, 0:c.OW], wb[s][:, i4 * 128:(i4 + 1) * 128],
                             av[:, kk, tcc * c.OW:(tcc + 1) * c.OW],
                             start=(kk == 0), stop=(kk == KD - 1),
                             rbufs=(bwb[s], bacts), wbuf=bpb[g])
            else:
                for jt in range(c.NT):
                    if kk == 0 and seed is not None:
                        sa, sbv, srb = seed(og, jt)
                        k.mm(pb[jt][:, 0:OWD], sa, sbv, start=True, stop=False,
                             rbufs=srb, wbuf=bpb[jt])
                    k.mm(pb[jt][:, 0:OWD], av[:, kk, jt * 128:(jt + 1) * 128],
                         wb[s][:, 0:OWD], start=(kk == 0 and seed is None),
                         stop=(kk == KD - 1), rbufs=(bwb[s], bacts), wbuf=bpb[jt])
        if feature_out:
            for i4 in range(nw):
                for tcc in range(c.NTC):
                    g = i4 * c.NTC + tcc
                    consume(og * nw + i4, tcc, pb[g], bpb[g])
        else:
            for jt in range(c.NT):
                consume(jt, og, pb[jt], bpb[jt])


def emit_ffn_quant(k: MK, inp):
    c = k.c
    nc = k.nc
    R = c.n_cores
    FS = c.DFF // R
    D = c.D
    ident = k.sb("ident_bf", [128, 128], BF16)
    bident = k.buf("ident_bf")
    onesf = k.sb("ones_f32", [128, 1], F32)
    bonesf = k.buf("ones_f32")
    wt = [k.sb("ln_xt0", [128, max(2048, k.c.D)], F32)] * 2
    bwt = [k.buf("ln_xt0")] * 2
    sqz = [k.sem("s_lnx0")] * 2
    acc = k.sb("qz_acc", [128, 2], F32)
    bacc = k.buf("qz_acc")
    col = k.sb("qz_col", [128, 1], F32)
    bcol = k.buf("qz_col")
    scs = k.sb("qz_scs", [1, 2], F32)
    bscs = k.buf("qz_scs")
    thr = k.sb("qz_thr", [128, 4], F32)
    bthr = k.buf("qz_thr")
    qa = k.sb("qz_qa", [128, 2048], BF16)
    bqa = k.buf("qz_qa")
    qb2 = k.sb("qz_qb2", [128, 2048], BF16)
    bqb2 = k.buf("qz_qb2")
    qq = k.sb("qz_qq", [128, 2048], BF16)
    bqq = k.buf("qz_qq")
    pb6 = k.ps("pb6", [128, 512], F32)
    bpb6 = k.buf("pb6")
    pb7 = k.ps("pb7", [128, 512], F32)
    pbt = pb7[:].bitcast(BF16)
    bpb7 = k.buf("pb7")
    PAN1 = k.sb("A2", [128, 4 * 4096], BF16)
    bPAN1 = k.buf("A2")
    PAN2 = k.sb("A3", [128, 4 * 4096], BF16)
    bPAN2 = k.buf("A3")

    w1s, w2s = inp["w1s"], inp["w2s"]
    bw1s, bw2s = k.buf("d_w1s"), k.buf("d_w2s")
    sc_in = nc.dram_tensor("qz_scin", [1, 2], F32).ap()
    sc_out = nc.dram_tensor("qz_scout", [1, 2], F32).ap()
    bscin, bscout = k.buf("d_scin"), k.buf("d_scout")
    w1q_in = nc.dram_tensor("w1q_in", [D, FS], BF16).ap()
    w2q_in = nc.dram_tensor("w2q_in", [FS, D], BF16).ap()
    bw1qin, bw2qin = k.buf("d_w1qin"), k.buf("d_w2qin")
    w1q_g = nc.dram_tensor("w1q_g", [R * D, FS], BF16).ap()
    w2q_g = nc.dram_tensor("w2q_g", [R * FS, D], BF16).ap()
    bw1qg, bw2qg = k.buf("d_w1qg"), k.buf("d_w2qg")
    sdio = k.sem("s_qzio")
    spans = [k.sem("s_qzpan1"), k.sem("s_qzpan2")]

    k.vop("memset", (onesf[:], 1.0), reads=(), writes=(bonesf,))
    cnt = 0
    for wi, (wsrc, bsrc, rows, cols) in enumerate(((w1s, bw1s, FS, D), (w2s, bw2s, D, FS))):
        for r0 in range(0, rows, 128):
            nc2 = min(2048, cols)
            for c0 in range(0, cols, nc2):
                s = cnt % 2
                k.dma("sync", wt[s][:, 0:nc2], wsrc[r0:r0 + 128, c0:c0 + nc2], sqz[s],
                      reads=(bsrc,), writes=(bwt[s],))
                k.vop("tensor_reduce", (col[:], wt[s][:, 0:nc2]), reads=(bwt[s],),
                      writes=(bcol,), kw=dict(axis=AX.X, op=OP.add, apply_absolute_value=True))
                if cnt == 0:
                    k.vop("memset", (acc[:], 0.0), reads=(), writes=(bacc,))
                k.vop("scalar_tensor_tensor",
                      (acc[:, wi:wi + 1], col[:], 1.0, acc[:, wi:wi + 1], OP.mult, OP.add),
                      reads=(bcol, bacc), writes=(bacc,))
                cnt += 1
    k.mm(pb6[0:1, 0:2], onesf[:], acc[:], start=True, stop=True,
         rbufs=(bonesf, bacc), wbuf=bpb6)
    k.act(scs[:], pb6[0:1, 0:2], AF.Copy, reads=(bpb6,), writes=(bscs,))
    k.dma("gpsimd", sc_in[:, :], scs[:], sdio, reads=(bscs,), writes=(bscin,))
    k.collective("AllReduce", OP.add, [list(range(R))], sc_in.opt(), sc_out.opt(),
                 bscin, bscout, "scale")
    k.dma("gpsimd", thr[:, 0:2], sc_out[0:1, :].partition_broadcast(128).squeeze(1), sdio,
          reads=(bscout,), writes=(bthr,))
    k.vop("tensor_scalar", (thr[:, 2:4], thr[:, 0:2], -0.7 / (c.DFF * D), None,
                            OP.mult, OP.bypass), reads=(bthr,), writes=())
    k.vop("tensor_scalar", (thr[:, 0:2], thr[:, 0:2], 0.7 / (c.DFF * D), None,
                            OP.mult, OP.bypass), reads=(bthr,), writes=(bthr,))
    for wi, (wsrc, bsrc, rows, cols, pan, bpan, q_in, bq_in, npan_cols) in enumerate((
            (w1s, bw1s, FS, D, PAN1, bPAN1, w1q_in, bw1qin, FS),
            (w2s, bw2s, D, FS, PAN2, bPAN2, w2q_in, bw2qin, D))):
        npan = cols // 128
        for r0 in range(0, rows, 128):
            s = (r0 // 128) % 2
            k.dma("sync", wt[s][:, 0:cols] if cols <= 2048 else None,
                  wsrc[r0:r0 + 128, 0:cols], sqz[s], reads=(bsrc,), writes=(bwt[s],))
            k.vop("tensor_scalar", (qa[:, 0:cols], wt[s][:, 0:cols], thr[:, wi:wi + 1], None,
                                    OP.is_ge, OP.bypass), reads=(bwt[s], bthr), writes=(bqa,))
            k.vop("tensor_scalar", (qb2[:, 0:cols], wt[s][:, 0:cols], thr[:, 2 + wi:3 + wi],
                                    None, OP.is_le, OP.bypass), reads=(bwt[s], bthr), writes=(bqb2,))
            k.vop("tensor_tensor", (qq[:, 0:cols], qa[:, 0:cols], qb2[:, 0:cols], OP.subtract),
                  reads=(bqa, bqb2), writes=(bqq,))
            for g in range(0, npan, 4):
                n4 = min(4, npan - g)
                lastw = None
                for j in range(n4):
                    lastw = k.op("tensor",
                                 lambda e, o=pbt[:, j * 128:(j + 1) * 128],
                                 i=qq[:, (g + j) * 128:(g + j + 1) * 128], d=ident[:]:
                                 e.transpose(o, i, d),
                                 reads=(bqq, bident), writes=(),
                                 xw=(bpb7.writers + bpb7.readers) if j == 0 else ())
                bpb7.writers = [lastw]
                bpb7.readers = []
                dst = pan[:].rearrange("p (i t) -> p i t", i=npan)[:, g:g + n4, r0:r0 + 128]
                src = pbt.rearrange("p (i t) -> p i t", i=8)[:, 0:n4, 0:128]
                wr = (bpan,) if (r0 + 128 >= rows and g + 4 >= npan) else ()
                k.act(dst, src, AF.Copy, reads=(bpb7,), writes=wr)
        sp = spans[wi]
        for p0 in range(npan):
            k.dma("scalar", q_in[p0 * 128:(p0 + 1) * 128, 0:rows],
                  pan[:].rearrange("p (i t) -> p i t", i=npan)[:, p0, 0:rows], sp,
                  reads=(bpan,), writes=(bq_in,) if p0 == npan - 1 else ())
        bq_in.writers = [(sp, sp.n)]
        bq_in.readers = []
    k.collective("AllGather", OP.bypass, [list(range(R))], w1q_in.opt(), w1q_g.opt(),
                 bw1qin, bw1qg, "w1g")
    k.collective("AllGather", OP.bypass, [list(range(R))], w2q_in.opt(), w2q_g.opt(),
                 bw2qin, bw2qg, "w2g")
    return w1q_g, bw1qg, w2q_g, bw2qg


def emit_consts(k: MK, inp):
    c = k.c
    nc = k.nc
    d = {}
    CB = 6 * 128 + (c.NCHO + 1) * 128 + 2 * c.TOKS
    cbf = k.sb("cbf", [128, CB], BF16)
    bcbf = k.buf("cbf")
    scbf = k.sem("s_cbf")
    cf = k.sb("cf32", [128, 132], F32)
    bcf = k.buf("cf32")
    scf = k.sem("s_cf32")
    k.dma("sync", cbf[:], inp["cbf_in"][:, :], scbf, reads=(k.buf("d_cbf"),), writes=(bcbf,))
    k.dma("sync", cf[:], inp["cf32_in"][:, :], scf, reads=(k.buf("d_cf32"),), writes=(bcf,))
    identb = k.sb("ident_bf", [128, 128], BF16)
    bidentb = k.buf("ident_bf")
    k.vop("tensor_copy", (identb[:], cbf[:, 0:128]), reads=(bcbf,), writes=(bidentb,))
    d["maskS"] = cbf[:, 128:256]
    d["maskST"] = cbf[:, 256:384]
    d["incT"] = cbf[:, 384:512]
    d["onesrow"] = cbf[0:1, 512:640]
    d["onescol"] = cbf[:, 640:641]
    d["wcol2"] = cbf[:, 641:642]
    d["rtcol"] = cbf[:, 642:643]
    d["bcbf"] = bcbf
    M0 = 6 * 128
    d["maskrev"] = cbf[:, M0:M0 + (c.NCHO + 1) * 128]
    d["cosT"] = cbf[:, M0 + (c.NCHO + 1) * 128:M0 + (c.NCHO + 1) * 128 + c.TOKS]
    d["sstar"] = cbf[:, M0 + (c.NCHO + 1) * 128 + c.TOKS:M0 + (c.NCHO + 1) * 128 + 2 * c.TOKS]
    d["identf"] = cf[:, 0:128]
    d["sel0"] = cf[:, 128:129]
    d["sel1"] = cf[:, 129:130]
    d["epscol"] = cf[:, 130:131]
    d["onecol"] = cf[:, 131:132]
    d["bcf"] = bcf
    NB1 = c.DFF // 128
    b1t = k.sb("b1t", [NB1, 128], F32)
    bb1t = k.buf("b1t")
    sb1 = k.sem("s_b1")
    k.dma("sync", b1t[:], inp["b1"].rearrange("(a b) -> a b", b=128), sb1,
          reads=(k.buf("d_b1"),), writes=(bb1t,))
    sb2 = k.sem("s_b2")
    sbo = k.sem("s_bo")
    pb6 = k.ps("pb6", [128, 512], F32)
    bpb6 = k.buf("pb6")
    k.tp(pb6[:, 0:NB1], b1t[:], cf[0:NB1, 0:NB1],
         rbufs=(bb1t, bcf), wbuf=bpb6)
    b1col = k.sb("b1col", [128, NB1], F32)
    bb1col = k.buf("b1col")
    k.act(b1col[:], pb6[:, 0:NB1], AF.Copy, reads=(bpb6,), writes=(bb1col,))
    d["b1col"] = b1col
    d["bb1col"] = bb1col
    xt0 = k.sb("ln_xt0", [128, max(2048, c.D)], F32)
    bxt0 = k.buf("ln_xt0")
    brows = k.sb("biasrows", [1, 2 * c.D], BF16)
    bbrows = k.buf("biasrows")
    b2r = xt0[0:1, 0:c.D]
    k.dma("sync", b2r, inp["b2"].rearrange("(o a) -> o a", o=1), sb2,
          reads=(k.buf("d_b2"),), writes=(bxt0,))
    k.vop("tensor_copy", (brows[0:1, 0:c.D], b2r), reads=(bxt0,), writes=())
    bor = xt0[0:1, 0:c.D]
    k.dma("sync", bor, inp["bo"].rearrange("(o a) -> o a", o=1), sbo,
          reads=(k.buf("d_bo"),), writes=(bxt0,))
    k.vop("tensor_copy", (brows[0:1, c.D:2 * c.D], bor), reads=(bxt0,),
          writes=(bbrows,))
    d["b2rb"], d["bb2rb"] = brows[0:1, 0:c.D], bbrows
    d["borb"], d["bborb"] = brows[0:1, c.D:2 * c.D], bbrows
    return d


def _wap(w, OWD):
    return lambda kk, og: w[kk * 128:(kk + 1) * 128, og * OWD:(og + 1) * OWD]


def _store_feature_cb(k: MK, dram_ap, dram_buf, dram_sem, extra=None):
    c = k.c
    stg = [k.sb(f"fst_{i}", [128, 512], BF16) for i in range(2)]
    bstg = [k.buf(f"fst_{i}") for i in range(2)]
    state = {"n": 0}

    def cb(i, tc, bank, bbank):
        s = state["n"] % 2
        state["n"] += 1
        k.act(stg[s][:, 0:c.OW], bank[:, 0:c.OW], AF.Copy, reads=(bbank,), writes=(bstg[s],))
        k.dma("scalar", dram_ap[i * 128:(i + 1) * 128, tc * c.OW:(tc + 1) * c.OW],
              stg[s][:, 0:c.OW], dram_sem, reads=(bstg[s],), writes=(dram_buf,))
        if extra is not None:
            extra(i, tc, bank, bbank, stg[s], bstg[s])
    return cb


def _store_token_cb(k: MK, dram_ap, dram_buf, dram_sem, OWD):
    c = k.c
    stg = [k.sb(f"tst_{i}", [128, 512], BF16) for i in range(2)]
    bstg = [k.buf(f"tst_{i}") for i in range(2)]
    state = {"n": 0}

    def cb(jt, og, bank, bbank):
        s = state["n"] % 2
        state["n"] += 1
        k.act(stg[s][:, 0:OWD], bank[:, 0:OWD], AF.Copy, reads=(bbank,), writes=(bstg[s],))
        k.dma("scalar", dram_ap[jt * 128:(jt + 1) * 128, og * OWD:(og + 1) * OWD],
              stg[s][:, 0:OWD], dram_sem, reads=(bstg[s],), writes=(dram_buf,))
    return cb


def emit_rope(k: MK, arena, barena, nheads, co, cf):
    c = k.c
    M0 = 6 * 128 + (c.NCHO + 1) * 128
    cos0, ss0 = M0, M0 + c.TOKS
    cbf = k.sb("cbf", [128, 6 * 128 + (c.NCHO + 1) * 128 + 2 * c.TOKS], BF16)
    hT1 = k.sb("dl_ksmT", [128, max(c.D, (c.NCHO + 1) * 128, 2 * c.TOKS)], BF16)
    bt1 = k.buf("dl_ksmT")
    hT2 = k.sb("dl_kbT", [128, max(c.D, 2 * c.TOKS)], BF16)
    brot = k.buf("dl_kbT")
    t1 = hT1[0:128, 0:2 * c.TOKS].bitcast(F32)
    rot = hT2[:, 0:c.TOKS]
    for h in range(nheads):
        lo, hi = h * c.TOKS, (h + 1) * c.TOKS
        k.act(hT2[0:64, 0:c.TOKS], arena[64:128, lo:hi], AF.Copy,
              reads=(barena,), writes=())
        k.act(hT2[64:128, 0:c.TOKS], arena[0:64, lo:hi], AF.Copy,
              reads=(barena,), writes=(brot,))
        brot.writers = [(k.es["scalar"], k.es["scalar"].n)]
        brot.readers = []
        k.vop("tensor_tensor", (t1, arena[:, lo:hi],
                                cbf[:, cos0:cos0 + c.TOKS], OP.mult),
              reads=(barena, co["bcbf"]), writes=(bt1,))
        k.vop("tensor_tensor", (rot, rot, cbf[:, ss0:ss0 + c.TOKS], OP.mult),
              reads=(brot, co["bcbf"]), writes=(brot,))
        k.vop("tensor_tensor", (arena[:, lo:hi], t1, rot, OP.add),
              reads=(bt1, brot), writes=(barena,))


def emit_attention(k: MK, co, HQ, kv_of, KH, kT_g, bkT_g, v_g, bv_g, q_dram, bq_dram,
                   q_arena, bq_arena, wcol_ap, out_cb):
    c = k.c
    cbf = k.sb("cbf", [128, 6 * 128 + (c.NCHO + 1) * 128 + 2 * c.TOKS], BF16)
    cf = k.sb("cf32", [128, 132], F32)
    kpan = k.sb("qz_qa", [128, 2048], BF16)
    bkpan = k.buf("qz_qa")
    skp = k.sem("s_atkp")
    vpan = k.sb("qz_qb2", [128, 2048], BF16)
    bvpan = k.buf("qz_qb2")
    svp = k.sem("s_atvp")
    qq2 = k.sb("qz_qq", [128, 2048], BF16)
    qpan = qq2[:, 0:c.TOKS]
    bqpan = k.buf("at_qpan")
    bqpan.writers = list(k.buf("qz_qq").writers) + k.buf("qz_qq").readers
    sqp = k.sem("s_atqp")
    hSsfE = k.sb("dl_Ssf", [128, max(c.H * 128, 2 * c.TOKS)], F32)
    expb = [qq2[:, c.TOKS:2 * c.TOKS],
            hSsfE[:, c.TOKS:c.TOKS + c.TOKS // 2].bitcast(BF16)]
    bexpb = [k.buf("at_expb"), k.buf("at_expb2")]
    bexpb[0].writers = list(k.buf("qz_qq").writers) + k.buf("qz_qq").readers
    bexpb[1].writers = list(k.buf("dl_Ssf").writers) + k.buf("dl_Ssf").readers
    MW = (c.NCHO + 1) * 128
    hmsk0 = k.sb("dl_ksmT", [128, max(c.D, (c.NCHO + 1) * 128, 2 * c.TOKS)], BF16)
    hmsk1 = k.sb("dl_kbT", [128, max(c.D, 2 * c.TOKS)], BF16)
    msk = [hmsk0[:, 0:MW], hmsk1[:, 0:MW]]
    bmsk = [k.buf("at_msk0"), k.buf("at_msk1")]
    bmsk[0].writers = list(k.buf("dl_ksmT").writers) + k.buf("dl_ksmT").readers
    bmsk[1].writers = list(k.buf("dl_kbT").writers) + k.buf("dl_kbT").readers
    hSsf = k.sb("dl_Ssf", [128, max(c.H * 128, 2 * c.TOKS)], F32)
    rn = hSsf[0:1, 0:c.TOKS]
    brn = k.buf("at_rn")
    brn.writers = list(k.buf("dl_Ssf").writers) + k.buf("dl_Ssf").readers
    rnb = hSsf[0:1, c.TOKS + c.TOKS // 2:2 * c.TOKS].bitcast(BF16)
    brnb = k.buf("at_rnb")
    brnb.writers = list(brn.writers)
    hSb = k.sb("dl_Sb", [128, max(c.H * 128, c.TOKS + 512)], BF16)
    rb = hSb[:, 0:c.TOKS]
    brb = k.buf("at_rb")
    brb.writers = list(k.buf("dl_Sb").writers) + k.buf("dl_Sb").readers
    pb = [k.ps(f"pb{i}", [128, 512], F32) for i in range(8)]
    bpb = [k.buf(f"pb{i}") for i in range(8)]
    NSEG = (c.TOKS + 511) // 512
    scale = 1.0 / math.sqrt(128.0)
    SB = {0: (2, 3), 1: (6, 7)}

    for h in range(HQ):
        hkv = kv_of(h)
        if (h == 0) or (kv_of(h - 1) != hkv):
            last = None
            for sec in range(2):
                last = k.dma("sync", kpan[:, sec * c.TOKS:(sec + 1) * c.TOKS],
                             kT_g[sec * KH * 128 + hkv * 128:sec * KH * 128 + (hkv + 1) * 128, :],
                             skp, reads=(bkT_g,),
                             xw=(bkpan.writers + bkpan.readers) if sec == 0 else ())
            bkpan.writers = [last]
            bkpan.readers = []
            vv = v_g.rearrange("(a p) d -> p a d", p=128)
            nq = max(1, c.NCH // 4)
            for qd in range(0, c.NCH, nq):
                last = k.dma("sync",
                             vpan[:, 0:c.NCH * 128].rearrange("p (a e) -> p a e", a=c.NCH)[:, qd:qd + nq, :],
                             vv[:, qd:qd + nq, hkv * 128:(hkv + 1) * 128],
                             svp, reads=(bv_g,),
                             xw=(bvpan.writers + bvpan.readers) if qd == 0 else ())
            bvpan.writers = [last]
            bvpan.readers = []
        if q_dram is not None:
            k.dma("sync", qpan, q_dram[h * 128:(h + 1) * 128, :], sqp,
                  reads=(bq_dram,), writes=(bqpan,))

            def qv(a, b):
                return qpan[0:128, a:b]
            bqv = bqpan
        else:
            def qv(a, b, h=h):
                return q_arena[:, h * c.TOKS + a:h * c.TOKS + b]
            bqv = bq_arena

        def geom(kb):
            jlo = max(0, kb - c.NCHO)
            return jlo, jlo * 128, min(c.NT - 1, kb), (min(c.NT - 1, kb) + 1) * 128

        def do_score(kb):
            par = kb % 2
            b0, b1 = SB[par]
            jlo, qlo, jhi, mend = geom(kb)
            e0 = 6 * 128 + (c.NCHO - kb + jlo) * 128
            first_m = True
            for s in range(NSEG):
                s0, s1 = s * 512, min((s + 1) * 512, c.TOKS)
                if s1 <= qlo:
                    continue
                a0 = max(s0, qlo)
                bank = pb[b0 + s]
                bbank = bpb[b0 + s]
                k.mm(bank[:, a0 - s0:s1 - s0], kpan[:, kb * 128:(kb + 1) * 128],
                     qv(a0, s1), start=True, stop=True, rbufs=(bkpan, bqv), wbuf=bbank)
                m1 = min(s1, mend)
                if a0 < m1:
                    k.vop("tensor_tensor",
                          (msk[par][:, a0 - qlo:m1 - qlo], bank[:, a0 - s0:m1 - s0],
                           cbf[:, e0 + (a0 - qlo):e0 + (m1 - qlo)], OP.add),
                          reads=(bbank, co["bcbf"]),
                          writes=(bmsk[par],) if m1 == mend else (),
                          xw=(bmsk[par].writers + bmsk[par].readers) if first_m else ())
                    first_m = False
                    if m1 == mend:
                        bmsk[par].writers = [(k.es["vector"], k.es["vector"].n)]
                        bmsk[par].readers = []

        def do_pv(kb):
            par = kb % 2
            b0, b1 = SB[par]
            jlo, qlo, jhi, mend = geom(kb)
            first_e = True
            for s in range(NSEG):
                s0, s1 = s * 512, min((s + 1) * 512, c.TOKS)
                if s1 <= qlo:
                    continue
                a0 = max(s0, qlo)
                m1 = min(s1, mend)
                xw = (bexpb[par].writers + bexpb[par].readers) if first_e else ()
                if a0 < m1:
                    k.act(expb[par][:, a0:m1], msk[par][:, a0 - qlo:m1 - qlo], AF.Exp,
                          reads=(bmsk[par],), writes=(), scale=scale, xw=xw)
                    first_e = False
                    xw = ()
                if m1 < s1:
                    k.act(expb[par][:, max(a0, m1):s1],
                          pb[b0 + s][:, max(a0, m1) - s0:s1 - s0], AF.Exp,
                          reads=(bpb[b0 + s],), writes=(), scale=scale, xw=xw)
                    first_e = False
            bexpb[par].writers = [(k.es["scalar"], k.es["scalar"].n)]
            bexpb[par].readers = []
            for s in range(NSEG):
                s0, s1 = s * 512, min((s + 1) * 512, c.TOKS)
                if s1 <= qlo:
                    continue
                a0 = max(s0, qlo)
                lastkb = min(c.NCH - 1, c.NCHO + s1 // 128 - 1)
                k.mm(pb[s][:, a0 - s0:s1 - s0], vpan[:, kb * 128:(kb + 1) * 128],
                     expb[par][:, a0:s1], start=(kb == 0), stop=(kb == lastkb),
                     rbufs=(bvpan, bexpb[par]), wbuf=bpb[s])
                k.mm(pb[4 + s][0:1, a0 - s0:s1 - s0], wcol_ap, expb[par][:, a0:s1],
                     start=(kb == 0), stop=(kb == lastkb),
                     rbufs=(co["bcbf"], bexpb[par]), wbuf=bpb[4 + s])

        do_score(0)
        for kb in range(c.NCH):
            if kb + 1 < c.NCH:
                do_score(kb + 1)
            do_pv(kb)
        for s in range(NSEG):
            s0, s1 = s * 512, min((s + 1) * 512, c.TOKS)
            k.vop("reciprocal", (rn[0:1, s0:s1], pb[4 + s][0:1, 0:s1 - s0]),
                  reads=(bpb[4 + s],), writes=(),
                  xw=(brn.writers + brn.readers) if s == 0 else ())
        brn.writers = [(k.es["vector"], k.es["vector"].n)]
        brn.readers = []
        k.vop("tensor_copy", (rnb, rn), reads=(brn,), writes=(brnb,))
        for s in range(NSEG):
            s0, s1 = s * 512, min((s + 1) * 512, c.TOKS)
            k.mm(pb[6][:, 0:s1 - s0], cbf[0:1, 512:640], rnb[0:1, s0:s1],
                 start=True, stop=True, rbufs=(co["bcbf"], brnb), wbuf=bpb[6])
            k.act(rb[:, s0:s1], pb[6][:, 0:s1 - s0], AF.Copy, reads=(bpb[6],),
                  writes=(), xw=(brb.writers + brb.readers) if s == 0 else ())
        brb.writers = [(k.es["scalar"], k.es["scalar"].n)]
        brb.readers = []
        out_cb(h, pb, bpb, rb, brb)


def emit_delta(k: MK, co, qsm, bqsm, kfeat_g, bkfeat_g, ktok_g, bktok_g, vtok_g, bvtok_g,
               ocD, bocD):
    c = k.c
    H, W, D = c.H, c.H * 128, c.D
    cbf = k.sb("cbf", [128, 6 * 128 + (c.NCHO + 1) * 128 + 2 * c.TOKS], BF16)
    cf = k.sb("cf32", [128, 132], F32)
    identb = k.sb("ident_bf", [128, 128], BF16)
    bident = k.buf("ident_bf")
    sel0 = cf[:, 128:129]
    sel1 = cf[:, 129:130]
    maskS, maskST, incT = cbf[:, 128:256], cbf[:, 256:384], cbf[:, 384:512]
    onescol = cbf[:, 640:641]
    onesrow = cbf[0:1, 512:640]

    AXA = k.sb("A_xnT", [128, max(c.KD * c.TOKS, 8 * c.D)], BF16)
    bxnT = k.buf("A_xnT")

    NREG = max(c.KD * c.TOKS, 8 * c.D) // c.D
    DB = 1

    def region(i, name):
        bb = k.buf(name)
        bb.writers = list(bxnT.writers) + list(bxnT.readers)
        bb.readers = []
        return AXA[:, i * c.D:(i + 1) * c.D], bb

    regs = {}
    for par in range(DB):
        for idx, nm in enumerate(("kF", "kT", "vT", "eFT", "gC", "gFb", "ksmF", "kbF")):
            regs[(nm, par)] = region(idx + 8 * par, f"dlc_{nm}{par}")
    skFs = [k.sem("s_dlkF0"), k.sem("s_dlkF1")]
    skTs = [k.sem("s_dlkT0"), k.sem("s_dlkT1")]
    svTs = [k.sem("s_dlvT0"), k.sem("s_dlvT1")]
    gA = k.sb("qz_qa", [128, 2048], BF16)
    bgA = k.buf("qz_qa")
    gB = k.sb("qz_qb2", [128, 2048], BF16)
    bgB = k.buf("qz_qb2")
    tmpF = k.sb("dl_tmpF", [128, 512], F32)
    btmpF = k.buf("dl_tmpF")
    rnF = k.sb("dl_rnF", [1, 512], F32)
    brnF = k.buf("dl_rnF")
    rnFb = k.sb("dl_rnFb", [1, 512], BF16)
    brnFb = k.buf("dl_rnFb")
    rT = k.sb("dl_rT", [128, max(H, 2)], F32)
    brT = k.buf("dl_rT")
    rTi = k.sb("dl_rTi", [128, max(H, 2)], F32)
    brTi = k.buf("dl_rTi")
    ksmT = k.sb("dl_ksmT", [128, max(D, (c.NCHO + 1) * 128, 2 * c.TOKS)], BF16)
    bksmT = k.buf("dl_ksmT")
    kbT = k.sb("dl_kbT", [128, max(D, 2 * c.TOKS)], BF16)
    bkbT = k.buf("dl_kbT")
    NG = min(4, H)
    GW = NG * 128
    Lm = k.sb("dl_Lm", [128, GW], BF16)
    bLm = k.buf("dl_Lm")
    LTm = k.sb("dl_LTm", [128, GW], BF16)
    bLTm = k.buf("dl_LTm")
    XT0 = k.sb("dl_XT0", [128, GW], BF16)
    bXT0 = k.buf("dl_XT0")
    XT1 = k.sb("dl_ocs", [128, GW], BF16)
    bXT1 = k.buf("dl_ocs")
    P1b = k.sb("dl_P1", [128, GW], BF16)
    bP1 = k.buf("dl_P1")
    PT1b = k.sb("dl_PT1", [128, GW], BF16)
    bPT1 = k.buf("dl_PT1")
    P2b = k.sb("dl_P2", [128, GW], BF16)
    bP2 = k.buf("dl_P2")
    PT2b = k.sb("dl_PT2", [128, GW], BF16)
    bPT2 = k.buf("dl_PT2")
    XT2, bXT2 = P1b, bP1
    XT3, bXT3 = PT1b, bPT1
    wN = PT1b
    bwN = bPT1
    t4 = P1b
    bt4 = bP1
    ub = k.sb("dl_ub", [128, GW], BF16)
    bub = k.buf("dl_ub")
    atb = k.sb("dl_atb", [128, GW], BF16)
    batb = k.buf("dl_atb")
    to4 = k.sb("dl_to4", [128, GW], BF16)
    bto4 = k.buf("dl_to4")
    ocs, bocs = XT1, bXT1
    Ssf = k.sb("dl_Ssf", [128, max(W, 2 * c.TOKS)], F32)
    bSsf = k.buf("dl_Ssf")
    Sb = k.sb("dl_Sb", [128, max(W, c.TOKS + 512)], BF16)
    bSb = k.buf("dl_Sb")
    pb = [k.ps(f"pb{i}", [128, 512], F32) for i in range(8)]
    bpb = [k.buf(f"pb{i}") for i in range(8)]

    def q4(i):
        return slice(i * 128, (i + 1) * 128)

    k.vop("memset", (Ssf[:], 0.0), reads=(), writes=(bSsf,))
    k.vop("memset", (Sb[:], 0.0), reads=(), writes=(bSb,))

    kf_r = kfeat_g.rearrange("(a h p) t -> a p h t", a=2, p=128)
    EPS = c.EPS_SM
    for ch in range(c.NCH):
        par = ch % DB
        (dkF, bkF) = regs[("kF", par)]
        (dkT, bkT2) = regs[("kT", par)]
        (dvT, bvT2) = regs[("vT", par)]
        (eF, beF) = regs[("eFT", par)]
        (gC, bgC) = regs[("gC", par)]
        (gFb, bgFb) = regs[("gFb", par)]
        (ksmF, bksmF) = regs[("ksmF", par)]
        (kbF, bkbF) = regs[("kbF", par)]
        eT, beT = eF, beF
        gTb, bgTb = gFb, bgFb
        sec, ccol = ch // c.NCHO, ch % c.NCHO
        j = ch % c.NCHO
        ng4 = max(1, H // 4)
        last = None
        for g4 in range(0, H, ng4):
            last = k.dma("sync",
                         dkF.rearrange("p (h e) -> p h e", h=H)[:, g4:g4 + ng4, 0:128],
                         kf_r[sec, :, g4:g4 + ng4, ccol * 128:(ccol + 1) * 128],
                         skFs[par], reads=(bkfeat_g,),
                         xw=(bkF.writers + bkF.readers) if g4 == 0 else ())
        bkF.writers = [last]
        bkF.readers = []
        k.dma("sync", dkT, ktok_g[ch * 128:(ch + 1) * 128, :], skTs[par],
              reads=(bktok_g,), writes=(bkT2,))
        k.dma("sync", dvT, vtok_g[ch * 128:(ch + 1) * 128, :], svTs[par],
              reads=(bvtok_g,), writes=(bvT2,))
        k.act(eF, dkF, AF.Exp, reads=(bkF,), writes=(beF,))
        for sg in range(0, W, 512):
            sw = min(512, W - sg)
            k.mm(pb[6][0:1, 0:sw], onescol, eF[:, sg:sg + sw], start=True, stop=True,
                 rbufs=(co["bcbf"], beF), wbuf=bpb[6])
            k.vop("reciprocal", (rnF[0:1, 0:sw], pb[6][0:1, 0:sw]),
                  reads=(bpb[6],), writes=(brnF,))
            k.vop("tensor_copy", (rnFb[0:1, 0:sw], rnF[0:1, 0:sw]), reads=(brnF,),
                  writes=(brnFb,))
            k.mm(pb[6][:, 0:sw], onesrow, rnFb[0:1, 0:sw], start=True, stop=True,
                 rbufs=(co["bcbf"], brnFb), wbuf=bpb[6])
            k.vop("tensor_tensor", (tmpF[:, 0:sw], eF[:, sg:sg + sw], pb[6][:, 0:sw],
                                    OP.mult), reads=(beF, bpb[6]), writes=(btmpF,))
            k.vop("tensor_scalar", (ksmF[:, sg:sg + sw], tmpF[:, 0:sw], EPS, 1.0 - EPS,
                                    OP.max, OP.min), reads=(btmpF,),
                  writes=(),
                  xw=(bksmF.writers + bksmF.readers) if sg == 0 else ())
        bksmF.writers = [(k.es["vector"], k.es["vector"].n)]
        bksmF.readers = []
        k.act(gA[:, 0:W], dkF, AF.Abs, reads=(bkF,), writes=(bgA,))
        k.act(gB[:, 0:W], gA[:, 0:W], AF.Exp, reads=(bgA,), writes=(bgB,), scale=-1.0)
        k.act(gA[:, 0:W], gB[:, 0:W], AF.Ln, reads=(bgB, co["bcf"]), writes=(bgA,),
              bias=co["onecol"])
        k.vop("tensor_scalar", (gB[:, 0:W], dkF, 0.0, 1.0 / 16.0, OP.min, OP.mult),
              reads=(bkF,), writes=(bgB,))
        k.vop("scalar_tensor_tensor", (gC, gA[:, 0:W], -1.0 / 16.0, gB[:, 0:W],
                                       OP.mult, OP.add), reads=(bgA, bgB), writes=(bgC,))
        k.vop("tensor_scalar", (gFb, gC, -16.0, 16.0, OP.max, OP.min),
              reads=(bgC,), writes=(bgFb,))
        k.vop("tensor_tensor", (kbF, ksmF, gFb, OP.mult),
              reads=(bksmF, bgFb), writes=(bkbF,))
        k.act(eT, dkT, AF.Exp, reads=(bkT2,), writes=(beT,))
        k.vop("tensor_reduce", (rT[:, 0:H], eT.rearrange("p (h e) -> p h e", h=H)),
              reads=(beT,), writes=(brT,), kw=dict(axis=AX.X, op=OP.add))
        k.vop("reciprocal", (rTi[:, 0:H], rT[:, 0:H]), reads=(brT,), writes=(brTi,))
        for h in range(H):
            k.vop("tensor_scalar", (gC[:, h * 128:(h + 1) * 128],
                                    eT[:, h * 128:(h + 1) * 128], rTi[:, h:h + 1], None,
                                    OP.mult, OP.bypass), reads=(beT, brTi),
                  writes=(),
                  xw=(bgC.writers + bgC.readers) if h == 0 else ())
        bgC.writers = [(k.es["vector"], k.es["vector"].n)]
        bgC.readers = []
        k.vop("tensor_scalar", (ksmT[:, 0:D], gC, EPS, 1.0 - EPS, OP.max, OP.min),
              reads=(bgC,), writes=(bksmT,))
        k.act(gA[:, 0:D], dkT, AF.Abs, reads=(bkT2,), writes=(bgA,))
        k.act(gB[:, 0:D], gA[:, 0:D], AF.Exp, reads=(bgA,), writes=(bgB,), scale=-1.0)
        k.act(gA[:, 0:D], gB[:, 0:D], AF.Ln, reads=(bgB, co["bcf"]), writes=(bgA,),
              bias=co["onecol"])
        k.vop("tensor_scalar", (gB[:, 0:D], dkT, 0.0, 1.0 / 16.0, OP.min, OP.mult),
              reads=(bkT2,), writes=(bgB,))
        k.vop("scalar_tensor_tensor", (gC, gA[:, 0:D], -1.0 / 16.0, gB[:, 0:D],
                                       OP.mult, OP.add), reads=(bgA, bgB), writes=(bgC,))
        k.vop("tensor_scalar", (gTb, gC, -16.0, 16.0, OP.max, OP.min),
              reads=(bgC,), writes=(bgTb,))
        k.vop("tensor_tensor", (kbT[:, 0:D], ksmT[:, 0:D], gTb, OP.mult),
              reads=(bksmT, bgTb), writes=(bkbT,))
        k.vop("tensor_tensor", (dvT, dvT, gTb, OP.mult),
              reads=(bgTb,), writes=(bvT2,))
        for g0 in range(0, H, NG):
            gs = slice(g0 * 128, (g0 + NG) * 128)

            def hs(i, g0=g0):
                return slice((g0 + i) * 128, (g0 + i + 1) * 128)

            for i in range(NG):
                k.mm(pb[2][:, q4(i)], kbF[:, hs(i)], ksmF[:, hs(i)], start=True,
                     stop=True, rbufs=(bkbF, bksmF),
                     wbuf=bpb[2] if i == NG - 1 else k.buf("_nw"),
                     xw=(bpb[2].writers + bpb[2].readers) if i == 0 else ())
                k.mm(pb[3][:, q4(i)], ksmF[:, hs(i)], kbF[:, hs(i)], start=True,
                     stop=True, rbufs=(bksmF, bkbF),
                     wbuf=bpb[3] if i == NG - 1 else k.buf("_nw"),
                     xw=(bpb[3].writers + bpb[3].readers) if i == 0 else ())
            for i in range(NG):
                h = g0 + i
                k.mm(pb[6][:, q4(i)], Sb[:, hs(i)],
                     qsm[:, h * c.TOKS + j * 128:h * c.TOKS + (j + 1) * 128],
                     start=True, stop=True, rbufs=(bSb, bqsm),
                     wbuf=bpb[6] if i == NG - 1 else k.buf("_nw"),
                     xw=(bpb[6].writers + bpb[6].readers) if i == 0 else ())
            selc0 = sel0 if ch < c.NCHO else sel1
            k.act(to4[:], pb[6][:, 0:GW], AF.Copy, reads=(bpb[6], co["bcf"]),
                  writes=(bto4,), scale=selc0)
            for i in range(NG):
                h = g0 + i
                k.mm(pb[0][:, q4(i)], ksmF[:, hs(i)],
                     qsm[:, h * c.TOKS + j * 128:h * c.TOKS + (j + 1) * 128],
                     start=True, stop=True, rbufs=(bksmF, bqsm),
                     wbuf=bpb[0] if i == NG - 1 else k.buf("_nw"),
                     xw=(bpb[0].writers + bpb[0].readers) if i == 0 else ())
            k.vop("tensor_tensor", (atb[:].rearrange("p (a e) -> p a e", a=NG),
                                    pb[0][:, 0:GW].rearrange("p (a e) -> p a e", a=NG),
                                    incT.unsqueeze(1).to_broadcast([128, NG, 128]),
                                    OP.mult), reads=(bpb[0], co["bcbf"]),
                  writes=(batb,))
            mS3 = maskS.unsqueeze(1).to_broadcast([128, NG, 128])
            mST3 = maskST.unsqueeze(1).to_broadcast([128, NG, 128])
            id3 = identb[:].unsqueeze(1).to_broadcast([128, NG, 128])
            inc3 = incT.unsqueeze(1).to_broadcast([128, NG, 128])
            k.vop("tensor_tensor", (Lm[:].rearrange("p (a e) -> p a e", a=NG),
                                    pb[2][:, 0:GW].rearrange("p (a e) -> p a e", a=NG),
                                    mS3, OP.mult), reads=(bpb[2], co["bcbf"]),
                  writes=(bLm,))
            k.vop("tensor_tensor", (LTm[:].rearrange("p (a e) -> p a e", a=NG),
                                    pb[3][:, 0:GW].rearrange("p (a e) -> p a e", a=NG),
                                    mST3, OP.mult), reads=(bpb[3], co["bcbf"]),
                  writes=(bLTm,))
            k.vop("tensor_tensor", (XT0[:].rearrange("p (a e) -> p a e", a=NG), id3,
                                    LTm[:].rearrange("p (a e) -> p a e", a=NG),
                                    OP.subtract), reads=(bident, bLTm), writes=(bXT0,))
            for i in range(NG):
                k.mm(pb[4][:, q4(i)], LTm[:, q4(i)], Lm[:, q4(i)], start=True, stop=True,
                     rbufs=(bLTm, bLm), wbuf=bpb[4] if i == NG - 1 else k.buf("_nw"),
                     xw=(bpb[4].writers + bpb[4].readers) if i == 0 else ())
                k.mm(pb[5][:, q4(i)], Lm[:, q4(i)], LTm[:, q4(i)], start=True, stop=True,
                     rbufs=(bLm, bLTm), wbuf=bpb[5] if i == NG - 1 else k.buf("_nw"),
                     xw=(bpb[5].writers + bpb[5].readers) if i == 0 else ())
            k.act(P1b[:], pb[4][:, 0:GW], AF.Copy, reads=(bpb[4],), writes=(bP1,))
            k.act(PT1b[:], pb[5][:, 0:GW], AF.Copy, reads=(bpb[5],), writes=(bPT1,))
            for i in range(NG):
                k.mm(pb[6][:, q4(i)], P1b[:, q4(i)], XT0[:, q4(i)], start=True, stop=True,
                     rbufs=(bP1, bXT0), wbuf=bpb[6] if i == NG - 1 else k.buf("_nw"),
                     xw=(bpb[6].writers + bpb[6].readers) if i == 0 else ())
            k.vop("tensor_tensor", (XT1[:], pb[6][:, 0:GW], XT0[:], OP.add),
                  reads=(bpb[6], bXT0), writes=(bXT1,))
            for i in range(NG):
                k.mm(pb[4][:, q4(i)], PT1b[:, q4(i)], P1b[:, q4(i)], start=True, stop=True,
                     rbufs=(bPT1, bP1), wbuf=bpb[4] if i == NG - 1 else k.buf("_nw"),
                     xw=(bpb[4].writers + bpb[4].readers) if i == 0 else ())
                k.mm(pb[5][:, q4(i)], P1b[:, q4(i)], PT1b[:, q4(i)], start=True, stop=True,
                     rbufs=(bP1, bPT1), wbuf=bpb[5] if i == NG - 1 else k.buf("_nw"),
                     xw=(bpb[5].writers + bpb[5].readers) if i == 0 else ())
            k.act(P2b[:], pb[4][:, 0:GW], AF.Copy, reads=(bpb[4],), writes=(bP2,))
            k.act(PT2b[:], pb[5][:, 0:GW], AF.Copy, reads=(bpb[5],), writes=(bPT2,))
            for i in range(NG):
                k.mm(pb[6][:, q4(i)], P2b[:, q4(i)], XT1[:, q4(i)], start=True, stop=True,
                     rbufs=(bP2, bXT1), wbuf=bpb[6] if i == NG - 1 else k.buf("_nw"),
                     xw=(bpb[6].writers + bpb[6].readers) if i == 0 else ())
            k.vop("tensor_tensor", (XT2[:], pb[6][:, 0:GW], XT1[:], OP.add),
                  reads=(bpb[6], bXT1), writes=(bXT2,))
            XTf, bXTf = XT2, bXT2
            for i in range(NG):
                k.mm(pb[7][:, q4(i)], kbT[:, hs(i)], XTf[:, q4(i)], start=True, stop=True,
                     rbufs=(bkbT, bXTf), wbuf=bpb[7] if i == NG - 1 else k.buf("_nw"),
                     xw=(bpb[7].writers + bpb[7].readers) if i == 0 else ())
            k.act(wN[:], pb[7][:, 0:GW], AF.Copy, reads=(bpb[7],), writes=(bwN,),
                  scale=-1.0)
            for i in range(NG):
                k.mm(pb[6][:, q4(i)], XTf[:, q4(i)], dvT[:, hs(i)], start=True, stop=True,
                     rbufs=(bXTf, bvT2), wbuf=bpb[6] if i == NG - 1 else k.buf("_nw"),
                     xw=(bpb[6].writers + bpb[6].readers) if i == 0 else ())
                k.mm(pb[7][:, q4(i)], wN[:, q4(i)], Sb[:, hs(i)], start=True, stop=True,
                     rbufs=(bwN, bSb), wbuf=bpb[7] if i == NG - 1 else k.buf("_nw"),
                     xw=(bpb[7].writers + bpb[7].readers) if i == 0 else ())
            k.act(t4[:], pb[6][:, 0:GW], AF.Copy, reads=(bpb[6],), writes=(bt4,))
            k.vop("tensor_tensor", (ub[:], pb[7][:, 0:GW], t4[:], OP.add),
                  reads=(bpb[7], bt4), writes=(bub,))
            for i in range(NG):
                k.mm(pb[7][:, q4(i)], ub[:, q4(i)], atb[:, q4(i)], start=True, stop=True,
                     rbufs=(bub, batb), wbuf=bpb[7] if i == NG - 1 else k.buf("_nw"),
                     xw=(bpb[7].writers + bpb[7].readers) if i == 0 else ())
            selc = selc0
            oc3 = ocD[:, 0:H * c.TOKS].rearrange("p (h t) -> p h t", h=H)[
                :, g0:g0 + NG, j * 128:(j + 1) * 128]
            if ch < c.NCHO:
                k.vop("scalar_tensor_tensor",
                      (oc3, pb[7][:, 0:GW].rearrange("p (a e) -> p a e", a=NG), selc,
                       to4[:].rearrange("p (a e) -> p a e", a=NG), OP.mult, OP.add),
                      reads=(bpb[7], bto4, co["bcf"]), writes=(bocD,))
            else:
                k.vop("scalar_tensor_tensor",
                      (ocs[:].rearrange("p (a e) -> p a e", a=NG),
                       pb[7][:, 0:GW].rearrange("p (a e) -> p a e", a=NG), selc,
                       to4[:].rearrange("p (a e) -> p a e", a=NG), OP.mult, OP.add),
                      reads=(bpb[7], bto4, co["bcf"]), writes=(bocs,))
                k.vop("tensor_tensor",
                      (oc3, ocs[:].rearrange("p (a e) -> p a e", a=NG), oc3, OP.add),
                      reads=(bocs,), writes=(bocD,))
            for i in range(NG):
                k.mm(pb[3][:, q4(i)], ksmT[:, hs(i)], ub[:, q4(i)], start=True, stop=True,
                     rbufs=(bksmT, bub), wbuf=bpb[3] if i == NG - 1 else k.buf("_nw"),
                     xw=(bpb[3].writers + bpb[3].readers) if i == 0 else ())
            k.vop("tensor_tensor", (Ssf[:, gs], pb[3][:, 0:GW], Ssf[:, gs], OP.add),
                  reads=(bpb[3],), writes=(bSsf,))
            k.vop("tensor_copy", (Sb[:, gs], Ssf[:, gs]), reads=(bSsf,), writes=(bSb,))
    for (nm, par2), (_, bb) in regs.items():
        bxnT.readers = bxnT.readers + bb.writers + bb.readers


def emit_residual_store(k: MK, xprev, bxprev, xout, bxout, sxout, OWD):
    c = k.c
    xpc = [k.sb(f"rs_xp{i}", [128, 512], F32) for i in range(2)]
    bxpc = [k.buf(f"rs_xp{i}") for i in range(2)]
    sxp = [k.sem(f"s_rsxp{i}") for i in range(2)]
    stg = [k.sb(f"rs_st{i}", [128, 512], F32) for i in range(2)]
    bstg = [k.buf(f"rs_st{i}") for i in range(2)]
    xoc = [k.sb("rs_xo0", [128, 512], F32)] * 2
    bxoc = [k.buf("rs_xo0")] * 2
    state = {"n": 0}

    def cb(jt, og, bank, bbank):
        s = state["n"] % 2
        state["n"] += 1
        k.dma("sync", xpc[s][:, 0:OWD], xprev[jt * 128:(jt + 1) * 128,
                                              og * OWD:(og + 1) * OWD], sxp[s],
              reads=(bxprev,), writes=(bxpc[s],))
        k.act(stg[s][:, 0:OWD], bank[:, 0:OWD], AF.Copy, reads=(bbank,),
              writes=(bstg[s],))
        k.vop("tensor_tensor", (xoc[s][:, 0:OWD], stg[s][:, 0:OWD], xpc[s][:, 0:OWD],
                                OP.add), reads=(bstg[s], bxpc[s]), writes=(bxoc[s],))
        k.dma("scalar", xout[jt * 128:(jt + 1) * 128, og * OWD:(og + 1) * OWD],
              xoc[s][:, 0:OWD], sxout, reads=(bxoc[s],), writes=(bxout,))
    return cb


def stage_attn_a(k: MK, co, inp, xprev, bxprev, xout, bxout, sxout):
    c = k.c
    nc = k.nc
    PAIRS = [[2 * i, 2 * i + 1] for i in range(c.n_cores // 2)]
    xnT, bxnT = emit_ln_xnT(k, co, xprev, bxprev)
    kfeat_in = nc.dram_tensor("kfeat_in", [c.D, c.TOKS], BF16).ap()
    bkfi = k.buf("d_kfeat_in")
    kfeat_g = nc.dram_tensor("kfeat_g", [2 * c.D, c.TOKS], BF16).ap()
    bkfg = k.buf("d_kfeat_g")
    ktok_in = nc.dram_tensor("ktok_in", [c.TOKS, c.D], BF16).ap()
    bkti = k.buf("d_ktok_in")
    ktok_g = nc.dram_tensor("ktok_g", [c.SEQ, c.D], BF16).ap()
    bktg = k.buf("d_ktok_g")
    vtok_in = nc.dram_tensor("vtok_in", [c.TOKS, c.D], BF16).ap()
    bvti = k.buf("d_vtok_in")
    vtok_g = nc.dram_tensor("vtok_g", [c.SEQ, c.D], BF16).ap()
    bvtg = k.buf("d_vtok_g")
    qraw_d = nc.dram_tensor("qraw_d", [c.D, c.TOKS], BF16).ap()
    bqrd = k.buf("d_qraw_d")
    sst_kf = k.sem("s_st_kf")
    sst_kt = k.sem("s_st_kt")
    sst_vt = k.sem("s_st_vt")
    sst_qr = k.sem("s_st_qr")

    wk = inp["wk_a"]
    bwk = k.buf("d_wk_a")
    emit_projection(k, xnT, bxnT, _wap(wk, min(512, c.D)),
                    c.D, c.D, _store_feature_cb(k, kfeat_in, bkfi, sst_kf), True, False)
    k.collective("AllGather", OP.bypass, PAIRS, kfeat_in.opt(), kfeat_g.opt(), bkfi, bkfg,
                 "kfa")
    emit_projection(k, xnT, bxnT, _wap(wk, min(512, c.D)),
                    c.D, c.D, _store_token_cb(k, ktok_in, bkti, sst_kt, min(512, c.D)),
                    False, False)
    k.collective("AllGather", OP.bypass, PAIRS, ktok_in.opt(), ktok_g.opt(), bkti, bktg,
                 "kta")
    wv = inp["wv_a"]
    emit_projection(k, xnT, bxnT, _wap(wv, min(512, c.D)),
                    c.D, c.D, _store_token_cb(k, vtok_in, bvti, sst_vt, min(512, c.D)),
                    False, False)
    k.collective("AllGather", OP.bypass, PAIRS, vtok_in.opt(), vtok_g.opt(), bvti, bvtg,
                 "vta")
    A2 = k.sb("A2", [128, 4 * 4096], BF16)
    bA2 = k.buf("A2")
    qsm = A2
    cbf = k.sb("cbf", [128, 6 * 128 + (c.NCHO + 1) * 128 + 2 * c.TOKS], BF16)
    rtcol = cbf[:, 642:643]
    onesrow = cbf[0:1, 512:640]
    EPSL = c.EPS_SM / math.sqrt(128.0)
    EPSH = (1.0 - c.EPS_SM) / math.sqrt(128.0)
    wq = inp["wq_a"]
    emit_projection(k, xnT, bxnT, _wap(wq, min(512, c.D)), c.D, c.D,
                    _store_feature_cb(k, qraw_d, bqrd, sst_qr), True, False)
    qld = [k.sb("qs_ld0", [128, 512], BF16)] * 2
    bqld = [k.buf("qs_ld0")] * 2
    sqld = [k.sem("s_qsld0")] * 2
    ebuf = k.sb("qa_e", [128, 512], BF16)
    bebuf = k.buf("qa_e")
    rq = k.sb("dl_rnF", [1, 512], F32)
    brq = k.buf("dl_rnF")
    rqb = k.sb("dl_rnFb", [1, 512], BF16)
    brqb = k.buf("dl_rnFb")
    tq = k.sb("dl_tmpF", [128, 512], F32)
    btq = k.buf("dl_tmpF")
    pb = [k.ps(f"pb{i}", [128, 512], F32) for i in range(8)]
    bpb = [k.buf(f"pb{i}") for i in range(8)]
    nstep = 0
    for i in range(c.H):
        for tc in range(c.NTC):
            s = nstep % 2
            nstep += 1
            k.dma("sync", qld[s][:, 0:c.OW],
                  qraw_d[i * 128:(i + 1) * 128, tc * c.OW:(tc + 1) * c.OW], sqld[s],
                  reads=(bqrd,), writes=(bqld[s],))
            k.act(ebuf[:, 0:c.OW], qld[s][:, 0:c.OW], AF.Exp, reads=(bqld[s],),
                  writes=(bebuf,))
            g2 = nstep % 2
            k.mm(pb[g2][0:1, 0:c.OW], rtcol, ebuf[:, 0:c.OW], start=True, stop=True,
                 rbufs=(co["bcbf"], bebuf), wbuf=bpb[g2])
            k.vop("reciprocal", (rq[0:1, 0:c.OW], pb[g2][0:1, 0:c.OW]), reads=(bpb[g2],),
                  writes=(brq,))
            k.vop("tensor_copy", (rqb[0:1, 0:c.OW], rq[0:1, 0:c.OW]), reads=(brq,),
                  writes=(brqb,))
            k.mm(pb[g2][:, 0:c.OW], onesrow, rqb[0:1, 0:c.OW], start=True, stop=True,
                 rbufs=(co["bcbf"], brqb), wbuf=bpb[g2])
            k.vop("tensor_tensor", (tq[:, 0:c.OW], ebuf[:, 0:c.OW], pb[g2][:, 0:c.OW],
                                    OP.mult), reads=(bebuf, bpb[g2]), writes=(btq,))
            k.vop("tensor_scalar", (qsm[:, i * c.TOKS + tc * c.OW:i * c.TOKS +
                                        (tc + 1) * c.OW],
                                    tq[:, 0:c.OW], EPSL, EPSH, OP.max, OP.min),
                  reads=(btq,), writes=(bA2,))
    A3 = k.sb("A3", [128, 4 * 4096], BF16)
    bA3 = k.buf("A3")
    emit_delta(k, co, qsm, bA2, kfeat_g, bkfg, ktok_g, bktg, vtok_g, bvtg, A3, bA3)
    fin = k.sb("A_xnT", [128, max(c.KD * c.TOKS, 8 * c.D)], BF16)
    bfin = k.buf("A_xnT")
    hSb2 = k.sb("dl_Sb", [128, max(c.H * 128, c.TOKS + 512)], BF16)
    ots = hSb2[:, c.TOKS:c.TOKS + 512]
    bots = k.buf("aa_ots")
    bots.writers = list(k.buf("dl_Sb").writers) + k.buf("dl_Sb").readers
    cbf2 = k.sb("cbf", [128, 6 * 128], BF16)
    wcol2 = cbf2[:, 641:642]

    def out_cb(h, pb, bpb, rbv, brbv):
        for s2 in range((c.TOKS + 511) // 512):
            s0, s1 = s2 * 512, min((s2 + 1) * 512, c.TOKS)
            k.vop("tensor_tensor", (ots[:, 0:s1 - s0], pb[s2][:, 0:s1 - s0],
                                    rbv[:, s0:s1], OP.mult),
                  reads=(bpb[s2], brbv), writes=(bots,))
            k.vop("tensor_tensor", (fin[:, h * c.TOKS + s0:h * c.TOKS + s1],
                                    ots[:, 0:s1 - s0],
                                    A3[:, h * c.TOKS + s0:h * c.TOKS + s1], OP.add),
                  reads=(bots, bA3), writes=(bfin,))

    emit_attention(k, co, c.H, lambda h: h, c.H, kfeat_g, bkfg, vtok_g, bvtg,
                   qraw_d, bqrd, None, None, wcol2, out_cb)
    wo = inp["wo_a"]
    emit_projection(k, fin, bfin, _wap(wo, min(512, c.D)), c.D, c.D,
                    emit_residual_store(k, xprev, bxprev, xout, bxout, sxout,
                                        min(512, c.D)), False, False)


def stage_ffn(k: MK, co, inp, w1qg, bw1qg, w2qg, bw2qg, xprev, bxprev, xout, bxout, sxout,
              acc0, bacc0, acc1, bacc1):
    c = k.c
    nc = k.nc
    R = c.n_cores
    FS = c.DFF // R
    xnT, bxnT = emit_ln_xnT(k, co, xprev, bxprev)
    slab_ar = [k.sb("A2", [128, 4 * 4096], BF16), k.sb("A3", [128, 4 * 4096], BF16)]
    bslab = [k.buf("A2"), k.buf("A3")]
    b2rb = co["b2rb"]
    cbf = k.sb("cbf", [128, 6 * 128], BF16)
    onesrow = cbf[0:1, 512:640]
    KS = c.SLAB // 128
    w1r = w1qg.rearrange("(r d) f -> r d f", r=R)

    for sl in range(c.NSLAB):
        ar, bar = slab_ar[sl % 2], bslab[sl % 2]

        def h_cb(i, tc, bank, bbank, sl=sl, ar=ar, bar=bar):
            fglob = sl * KS + i
            k.act(ar[:, i * c.TOKS + tc * c.OW:i * c.TOKS + (tc + 1) * c.OW],
                  bank[:, 0:c.OW], AF.Gelu, reads=(bbank, co["bb1col"]), writes=(bar,),
                  bias=co["b1col"][:, fglob:fglob + 1])

        OWD1 = min(512, FS)

        def w1_ap(kk, og, sl=sl):
            fglob0 = sl * c.SLAB + og * OWD1
            r = fglob0 // FS
            floc = fglob0 - r * FS
            return w1r[r, kk * 128:(kk + 1) * 128, floc:floc + OWD1]

        emit_projection(k, xnT, bxnT, w1_ap, c.D, c.SLAB, h_cb, True, True,
                        owd=min(512, FS), wdram=bw1qg)

        xp = xprev if sl == 0 else (acc0 if sl % 2 == 1 else acc1)
        bxp = bxprev if sl == 0 else (bacc0 if sl % 2 == 1 else bacc1)
        xo = xout if sl == c.NSLAB - 1 else (acc0 if sl % 2 == 0 else acc1)
        bxo = bxout if sl == c.NSLAB - 1 else (bacc0 if sl % 2 == 0 else bacc1)
        seed = None
        if sl == 0:
            def seed(og, jt, OWD2=min(512, c.D)):
                return (onesrow, b2rb[0:1, og * OWD2:(og + 1) * OWD2],
                        (co["bcbf"], co["bb2rb"]))

        OWD2 = min(512, c.D)

        def w2_ap(kk, og, sl=sl):
            return w2qg[sl * c.SLAB + kk * 128:sl * c.SLAB + (kk + 1) * 128,
                        og * OWD2:(og + 1) * OWD2]

        emit_projection(k, ar, bar, w2_ap, c.SLAB, c.D,
                        emit_residual_store(k, xp, bxp, xo, bxo,
                                            sxout if sl == c.NSLAB - 1 else k.sem("s_acc"),
                                            OWD2),
                        False, True, wdram=bw2qg)


def stage_attn_b(k: MK, co, inp, xprev, bxprev, xout, bxout, sxout):
    c = k.c
    nc = k.nc
    PAIRS = [[2 * i, 2 * i + 1] for i in range(c.n_cores // 2)]
    cf = k.sb("cf32", [128, 132], F32)
    cbf = k.sb("cbf", [128, 6 * 128 + (c.NCHO + 1) * 128 + 2 * c.TOKS], BF16)
    xnT, bxnT = emit_ln_xnT(k, co, xprev, bxprev)
    kb_in = nc.dram_tensor("kb_in", [c.DKV, c.TOKS], BF16).ap()
    bkbi = k.buf("d_kb_in")
    kb_g = nc.dram_tensor("kb_g", [2 * c.DKV, c.TOKS], BF16).ap()
    bkbg = k.buf("d_kb_g")
    vb_in = nc.dram_tensor("vb_in", [c.TOKS, c.DKV], BF16).ap()
    bvbi = k.buf("d_vb_in")
    vb_g = nc.dram_tensor("vb_g", [c.SEQ, c.DKV], BF16).ap()
    bvbg = k.buf("d_vb_g")
    sst_kb = k.sem("s_st_kb")
    sst_vb = k.sem("s_st_vb")
    M0 = 6 * 128 + (c.NCHO + 1) * 128
    cos0, ss0 = M0, M0 + c.TOKS

    rk1 = k.sb("dl_tmpF", [128, 512], F32)
    brk1 = k.buf("dl_tmpF")
    rk2 = k.sb("rs_st0", [128, 512], F32)
    brk2 = k.buf("rs_st0")
    kstg = [k.sb(f"fst_{i}", [128, 512], BF16) for i in range(2)]
    bkstg = [k.buf(f"fst_{i}") for i in range(2)]
    kst = {"n": 0}

    kev = k.sb("qa_e", [128, 512], BF16)
    bkev = k.buf("qa_e")

    def k_cb(i, tc, bank, bbank):
        s = kst["n"] % 2
        kst["n"] += 1
        t0 = tc * c.OW
        k.act(kev[:, 0:c.OW], bank[:, 0:c.OW], AF.Copy, reads=(bbank,), writes=(bkev,))
        k.act(rk2[0:64, 0:c.OW], kev[64:128, 0:c.OW], AF.Copy, reads=(bkev,), writes=())
        k.act(rk2[64:128, 0:c.OW], kev[0:64, 0:c.OW], AF.Copy,
              reads=(bkev,), writes=(brk2,))
        k.vop("tensor_tensor", (rk2[:, 0:c.OW], rk2[:, 0:c.OW],
                                cbf[:, ss0 + t0:ss0 + t0 + c.OW], OP.mult),
              reads=(co["bcbf"],), writes=(brk2,))
        k.vop("tensor_tensor", (rk1[:, 0:c.OW], kev[:, 0:c.OW],
                                cbf[:, cos0 + t0:cos0 + t0 + c.OW], OP.mult),
              reads=(bkev, co["bcbf"]), writes=(brk1,))
        k.vop("tensor_tensor", (kstg[s][:, 0:c.OW], rk1[:, 0:c.OW], rk2[:, 0:c.OW], OP.add),
              reads=(brk1, brk2), writes=(bkstg[s],))
        k.dma("scalar", kb_in[i * 128:(i + 1) * 128, t0:t0 + c.OW], kstg[s][:, 0:c.OW],
              sst_kb, reads=(bkstg[s],), writes=(bkbi,))

    wkb = inp["wk_b"]
    emit_projection(k, xnT, bxnT, _wap(wkb, min(512, c.DKV)), c.D, c.DKV, k_cb,
                    True, False)
    k.collective("AllGather", OP.bypass, PAIRS, kb_in.opt(), kb_g.opt(), bkbi, bkbg, "kb")
    wvb = inp["wv_b"]
    emit_projection(k, xnT, bxnT, _wap(wvb, min(512, c.DKV)), c.D, c.DKV,
                    _store_token_cb(k, vb_in, bvbi, sst_vb, min(512, c.DKV)),
                    False, False)
    k.collective("AllGather", OP.bypass, PAIRS, vb_in.opt(), vb_g.opt(), bvbi, bvbg, "vb")
    A2 = k.sb("A2", [128, 4 * 4096], BF16)
    bA2 = k.buf("A2")
    qst = {"n": 0}

    def q_cb(i, tc, bank, bbank):
        k.act(A2[:, i * c.TOKS + tc * c.OW:i * c.TOKS + (tc + 1) * c.OW], bank[:, 0:c.OW],
              AF.Copy, reads=(bbank,), writes=(bA2,))

    wqb = inp["wq_b"]
    emit_projection(k, xnT, bxnT, _wap(wqb, min(512, c.D)), c.D, c.D, q_cb, True, False)
    emit_rope(k, A2, bA2, c.H, co, cf)
    A3 = k.sb("A3", [128, 4 * 4096], BF16)
    bA3 = k.buf("A3")
    onescol = cbf[:, 640:641]

    def out_cb(h, pb, bpb, rbv, brbv):
        for s2 in range((c.TOKS + 511) // 512):
            s0, s1 = s2 * 512, min((s2 + 1) * 512, c.TOKS)
            k.vop("tensor_tensor", (A3[:, h * c.TOKS + s0:h * c.TOKS + s1],
                                    pb[s2][:, 0:s1 - s0], rbv[:, s0:s1], OP.mult),
                  reads=(bpb[s2], brbv), writes=(bA3,))

    reps = c.H // c.KVB
    emit_attention(k, co, c.H, lambda h: h // reps, c.KVB, kb_g, bkbg, vb_g, bvbg,
                   None, None, A2, bA2, onescol, out_cb)
    onesrow = cbf[0:1, 512:640]
    borb = co["borb"]

    def seed(og, jt, OWD2=min(512, c.D)):
        return (onesrow, borb[0:1, og * OWD2:(og + 1) * OWD2], (co["bcbf"], co["bborb"]))

    wob = inp["wo_b"]
    emit_projection(k, A3, bA3, _wap(wob, min(512, c.D)), c.D, c.D,
                    emit_residual_store(k, xprev, bxprev, xout, bxout, sxout,
                                        min(512, c.D)), False, False, seed=seed)


def build_program(cfg: Cfg, want):
    k = MK(cfg)
    nc = k.nc
    c = cfg
    inp = {}
    inp["xs"] = nc.dram_tensor("xs", [c.TOKS, c.D], F32, kind="ExternalInput").ap()
    for nm, sh in (("wq_a", [c.D, c.D]), ("wk_a", [c.D, c.D]), ("wv_a", [c.D, c.D]),
                   ("wo_a", [c.D, c.D]), ("wq_b", [c.D, c.D]), ("wk_b", [c.D, c.DKV]),
                   ("wv_b", [c.D, c.DKV]), ("wo_b", [c.D, c.D])):
        inp[nm] = nc.dram_tensor(nm, sh, F32, kind="ExternalInput").ap()
    inp["w1s"] = nc.dram_tensor("w1s", [c.DFF // c.n_cores, c.D], F32,
                                kind="ExternalInput").ap()
    inp["w2s"] = nc.dram_tensor("w2s", [c.D, c.DFF // c.n_cores], F32,
                                kind="ExternalInput").ap()
    inp["b1"] = nc.dram_tensor("b1", [c.DFF], F32, kind="ExternalInput").ap()
    inp["b2"] = nc.dram_tensor("b2", [c.D], F32, kind="ExternalInput").ap()
    inp["bo"] = nc.dram_tensor("bo", [c.D], F32, kind="ExternalInput").ap()
    inp["cbf_in"] = nc.dram_tensor("cbf_in",
                                   [128, 6 * 128 + (c.NCHO + 1) * 128 + 2 * c.TOKS],
                                   BF16, kind="ExternalInput").ap()
    inp["cf32_in"] = nc.dram_tensor("cf32_in", [128, 132], F32,
                                    kind="ExternalInput").ap()
    out = nc.dram_tensor("out_x", [c.TOKS, c.D], F32, kind="ExternalOutput").ap()
    bout = k.buf("d_out")
    bxs = k.buf("d_xs")
    xd = [nc.dram_tensor(f"xd{i}", [c.TOKS, c.D], F32).ap() for i in range(3)]
    bxd = [k.buf(f"d_xd{i}") for i in range(3)]
    acc0 = nc.dram_tensor("acc0", [c.TOKS, c.D], F32).ap()
    bacc0 = k.buf("d_acc0")
    acc1 = nc.dram_tensor("acc1", [c.TOKS, c.D], F32).ap()
    bacc1 = k.buf("d_acc1")
    sxout = k.sem("s_xout")

    co = emit_consts(k, inp)
    if "f1" in want or "f2" in want:
        w1qg, bw1qg, w2qg, bw2qg = emit_ffn_quant(k, inp)

    chain = [(inp["xs"], bxs)]
    for st in want:
        nxt = (out, bout) if st == want[-1] else (xd[len(chain) - 1], bxd[len(chain) - 1])
        xp, bxp = chain[-1]
        if st == "a":
            stage_attn_a(k, co, inp, xp, bxp, nxt[0], nxt[1], sxout)
        elif st in ("f1", "f2"):
            stage_ffn(k, co, inp, w1qg, bw1qg, w2qg, bw2qg, xp, bxp, nxt[0], nxt[1],
                      sxout, acc0, bacc0, acc1, bacc1)
        elif st == "b":
            stage_attn_b(k, co, inp, xp, bxp, nxt[0], nxt[1], sxout)
        chain.append(nxt)
    k.barrier_wait("gpsimd", [(sxout, sxout.n)] + bout.writers)
    return k



def make_core_inputs(cfg: Cfg, core, x_full, wq_a, wk_a, wv_a, wo_a, wq_b, wk_b, wv_b,
                     wo_b, bo, w1, b1, w2, b2):
    c = cfg
    b, h = core // 2, core % 2
    d = {}
    d["xs"] = np.ascontiguousarray(x_full[b, h * c.TOKS:(h + 1) * c.TOKS, :], np.float32)
    d["wq_a"], d["wk_a"], d["wv_a"], d["wo_a"] = wq_a, wk_a, wv_a, wo_a
    d["wq_b"], d["wk_b"], d["wv_b"], d["wo_b"] = wq_b, wk_b, wv_b, wo_b
    FS = c.DFF // c.n_cores
    d["w1s"] = np.ascontiguousarray(w1[core * FS:(core + 1) * FS, :], np.float32)
    d["w2s"] = np.ascontiguousarray(w2[:, core * FS:(core + 1) * FS], np.float32)
    d["b1"], d["b2"], d["bo"] = b1, b2, bo
    cbf = np.zeros((128, 6 * 128 + (c.NCHO + 1) * 128 + 2 * c.TOKS), np.float32)
    cbf[:, 0:128] = np.eye(128)
    jj, ii = np.meshgrid(np.arange(128), np.arange(128), indexing="xy")
    cbf[:, 128:256] = (np.arange(128)[None, :] < np.arange(128)[:, None])
    cbf[:, 256:384] = (np.arange(128)[None, :] > np.arange(128)[:, None])
    cbf[:, 384:512] = (np.arange(128)[:, None] <= np.arange(128)[None, :])
    cbf[0, 512:640] = 1.0
    cbf[:, 640] = 1.0
    cbf[:, 641] = 2.0
    cbf[:, 642] = math.sqrt(128.0)
    M0 = 6 * 128
    for e in range(c.NCHO + 1):
        dd = c.NCHO - e
        rel = dd - c.NCHO * h
        if rel < 0:
            m = np.zeros((128, 128), np.float32)
        elif rel == 0:
            m = np.where(np.arange(128)[:, None] <= np.arange(128)[None, :], 0.0, NEG)
        else:
            m = np.full((128, 128), NEG, np.float32)
        cbf[:, M0 + e * 128:M0 + (e + 1) * 128] = m
    pos = (h * c.TOKS + np.arange(c.TOKS)).astype(np.float64)
    inv = 1.0 / (10000.0 ** (np.arange(0, 128, 2, dtype=np.float64) / 128.0))
    fr = np.outer(pos, inv)
    cosT = np.concatenate([np.cos(fr).T, np.cos(fr).T], 0).astype(np.float32)
    sstar = np.concatenate([-np.sin(fr).T, np.sin(fr).T], 0).astype(np.float32)
    Mc = M0 + (c.NCHO + 1) * 128
    cbf[:, Mc:Mc + c.TOKS] = cosT
    cbf[:, Mc + c.TOKS:Mc + 2 * c.TOKS] = sstar
    d["cbf_in"] = cbf.astype(BF16NP)
    cf = np.zeros((128, 132), np.float32)
    cf[:, 0:128] = np.eye(128)
    cf[:, 128] = 0.5 * (1 - h)
    cf[:, 129] = 0.5 * h
    cf[:, 130] = 1e-5
    cf[:, 131] = 1.0
    d["cf32_in"] = cf
    return d


def run_cfg(cfg: Cfg, in_maps, sim=False):
    k = build_program(cfg, cfg.stages)
    nc = k.build()
    if sim:
        from concourse.bass_interp import MultiCoreSim
        s = MultiCoreSim(nc, num_cores=cfg.n_cores, num_workers=1)
        for i in range(cfg.n_cores):
            for kk, v in in_maps[i].items():
                s.cores[i].tensor(kk)[:] = v
        s.simulate(check_with_hw=False)
        return [{"out_x": np.asarray(s.cores[i].mem_tensor("out_x"))}
                for i in range(cfg.n_cores)]
    from concourse.bass_utils import run_bass_kernel_spmd
    res = run_bass_kernel_spmd(nc, in_maps, core_ids=list(range(cfg.n_cores)))
    return res.results


def kernel(L0_attn_a_x, L0_attn_a_Wq, L0_attn_a_Wk, L0_attn_a_Wv, L0_attn_a_Wo,
           L0_attn_b_Wq, L0_attn_b_Wk, L0_attn_b_Wv, L0_attn_b_Wo, L0_attn_b_bo,
           L0_ffn_w1, L0_ffn_b1, L0_ffn_w2, L0_ffn_b2):
    cfg = Cfg()
    x = np.asarray(L0_attn_a_x, np.float32)
    args = (x, np.asarray(L0_attn_a_Wq, np.float32), np.asarray(L0_attn_a_Wk, np.float32),
            np.asarray(L0_attn_a_Wv, np.float32), np.asarray(L0_attn_a_Wo, np.float32),
            np.asarray(L0_attn_b_Wq, np.float32), np.asarray(L0_attn_b_Wk, np.float32),
            np.asarray(L0_attn_b_Wv, np.float32), np.asarray(L0_attn_b_Wo, np.float32),
            np.asarray(L0_attn_b_bo, np.float32), np.asarray(L0_ffn_w1, np.float32),
            np.asarray(L0_ffn_b1, np.float32), np.asarray(L0_ffn_w2, np.float32),
            np.asarray(L0_ffn_b2, np.float32))
    in_maps = [make_core_inputs(cfg, c, args[0], *args[1:10],
                                args[10], args[11], args[12], args[13])
               for c in range(cfg.n_cores)]
    results = run_cfg(cfg, in_maps, sim=False)
    out = np.empty((4, 2048, 2048), np.float32)
    for c in range(cfg.n_cores):
        b, h = c // 2, c % 2
        out[b, h * cfg.TOKS:(h + 1) * cfg.TOKS, :] = results[c]["out_x"]
    return out
